# version 14
# baseline (speedup 1.0000x reference)
import numpy as np
import ml_dtypes
from contextlib import ExitStack

import concourse.bass as bass
import concourse.mybir as mybir
from concourse.bass_utils import run_bass_kernel_spmd

F32 = mybir.dt.float32
BF16 = mybir.dt.bfloat16
I32 = mybir.dt.int32
AF = mybir.ActivationFunctionType
ALU = mybir.AluOpType

P = 128
S = 8192
D = 2048
H = 8192
NCORE = 8
FSL = 512
NT = 16
NQ = 4
QTOK = S // 4
CSHIFT = 16.0
INV_SQRT_HD = 0.08838834764831845

ENGS = ("sync", "tensor", "vector", "scalar", "gpsimd")


class Ctr:
    __slots__ = ("n",)

    def __init__(self):
        self.n = 0

    def bump(self, k):
        self.n += k
        return self.n


ENG_SEMS = ("pe", "act", "dve", "pool", "cc")
DMA_SEMS = ("x0", "x1", "x2", "x3", "wdq", "kv0", "kv1", "q0", "q1", "wu0", "wu1",
            "x2b", "x2c", "wd0", "wd1", "pa0", "pa1", "wo_s",
            "st_q", "st_k", "st_v", "st_a", "st_x2", "st_y2", "st_o")
ALL_SEMS = ENG_SEMS + DMA_SEMS


def build_program(nphases=4, dbg=()):
    nc = bass.Bass("TRN2", target_bir_lowering=False)
    ops = {e: [] for e in ENGS}
    ctr = {s: Ctr() for s in ALL_SEMS}
    H_ = {}

    def wait(eng, sem, val):
        if val and val > 0:
            ops[eng].append(lambda e, s=sem, v=val: e.wait_ge(H_[s], v))

    def do(eng, fn, inc=None):
        if inc is None:
            ops[eng].append(fn)
            return None
        amt = 1 if inc in ENG_SEMS else 16
        c = ctr[inc].bump(amt)
        ops[eng].append(lambda e, f=fn, s=inc, a=amt: f(e).then_inc(H_[s], a))
        return c

    def inp(name, shape, dt):
        return nc.dram_tensor(name, list(shape), dt, kind="ExternalInput").ap()

    x_b = inp("x_b", (S, D), F32)
    x_q = inp("x_q", (QTOK, D), F32)
    wq_c = inp("wq_c", (D, FSL), F32)
    wk_c = inp("wk_c", (D, FSL), F32)
    wv_c = inp("wv_c", (D, FSL), F32)
    wq_se = inp("wq_se", (D, FSL), F32)
    wk_se = inp("wk_se", (D, FSL), F32)
    wv_se = inp("wv_se", (D, FSL), F32)
    wo_c = inp("wo_c", (D, D), F32)
    wo_se = inp("wo_se", (D, D), F32)
    bq_t = inp("bq_t", (P, 4), F32)
    bk_t = inp("bk_t", (P, 4), F32)
    bv_b = inp("bv_b", (P, FSL), F32)
    bo_b = inp("bo_b", (P, D), F32)
    bdn_b = inp("bdn_b", (P, D), F32)
    bup_t = inp("bup_t", (P, 64), F32)
    wupT_q = inp("wupT_q", (D, H), I32)
    wup_se = inp("wup_se", (D, H), F32)
    wdnT_q = inp("wdnT_q", (H, D), I32)
    wdn_se = inp("wdn_se", (H, D), F32)
    aup = inp("aup", (D, 16), F32)
    bupl = inp("bupl", (16, H), F32)
    adn = inp("adn", (H, 16), F32)
    bdnl = inp("bdnl", (16, D), F32)
    eye_in = inp("eye_bf", (P, P), BF16)
    qoff = inp("qoff", (1, 1), I32)

    out = nc.dram_tensor("out", [QTOK, D], F32, kind="ExternalOutput").ap()

    def scratch(name, shape, dt):
        kind = "ExternalOutput" if name in dbg else "Internal"
        return nc.dram_tensor(name, list(shape), dt, kind=kind).ap()

    q_slab = scratch("q_slab", (FSL, S), BF16)
    k_slab = scratch("k_slab", (FSL, S), BF16)
    v_slab = scratch("v_slab", (S, FSL), BF16)
    attn_l = scratch("attn_l", (FSL, S), BF16)
    attn_g = scratch("attn_g", (4 * FSL, S), BF16)
    x2_slab = scratch("x2_slab", (H, D), BF16)
    y2_slab = scratch("y2_slab", (QTOK, D), BF16)

    with ExitStack() as ctx:
        NA = 102400
        arena = ctx.enter_context(nc.sbuf_tensor("arena", [P, NA], BF16))
        arena_ap = arena[:, :]
        arena_f = arena[:, :].bitcast(F32)
        arena_i = arena[:, :].bitcast(I32)

        def bf(off_b, n, parts=P):
            assert off_b % 2 == 0 and off_b + 2 * n <= 2 * NA
            return arena_ap[0:parts, off_b // 2: off_b // 2 + n]

        def f32(off_b, n, parts=P):
            assert off_b % 4 == 0 and off_b + 4 * n <= 2 * NA
            return arena_f[0:parts, off_b // 4: off_b // 4 + n]

        def i32(off_b, n, parts=P):
            assert off_b % 4 == 0 and off_b + 4 * n <= 2 * NA
            return arena_i[0:parts, off_b // 4: off_b // 4 + n]

        px0 = ctx.enter_context(nc.psum_tensor("px0", [P, 1024], F32))
        px1 = ctx.enter_context(nc.psum_tensor("px1", [P, 1024], F32))
        px2 = ctx.enter_context(nc.psum_tensor("px2", [P, 1024], F32))
        pv_t = ctx.enter_context(nc.psum_tensor("pv_t", [P, 512], F32))
        pz = ctx.enter_context(nc.psum_tensor("pz", [P, 512], F32))
        px2bf = px2[:, :].bitcast(BF16)
        for s in ALL_SEMS:
            H_[s] = ctx.enter_context(nc.semaphore(s))

        bank_free = {}

        def bank_wait(eng, key):
            sc = bank_free.get(key)
            if sc:
                wait(eng, sc[0], sc[1])

        def bank_set(key, sem, count):
            bank_free[key] = (sem, count)

        O_SMALL = 0
        eye = bf(0, P)
        o = 256
        ones_bf = bf(o, 1); o += 64
        negC = f32(o, 1); o += 64
        eps = f32(o, 1); o += 64
        ones1f = f32(o, 128, parts=1); o += 512
        onescol = f32(o, 1); o += 64
        invD = f32(o, 1); o += 64
        stat = []
        for j in range(4):
            d = {}
            for nm in ("sum", "ssq", "m", "vart", "msq", "var2", "rstd", "mr"):
                d[nm] = f32(o, 1); o += 64
            stat.append(d)
        assert o <= 4096

        c_eye = do("sync", lambda x: x.dma_start(out=eye, in_=eye_in), "q0")
        do("vector", lambda x: x.memset(ones_bf, 1.0), "dve")
        do("vector", lambda x: x.memset(negC, -CSHIFT), "dve")
        do("vector", lambda x: x.memset(eps, 1e-5), "dve")
        do("vector", lambda x: x.memset(ones1f, 1.0), "dve")
        do("vector", lambda x: x.memset(onescol, 1.0), "dve")
        c_consts = do("vector", lambda x: x.memset(invD, 1.0 / D), "dve")

        O_XQ = 4096
        B1 = O_XQ + 65536
        O_WQD, O_WKD, O_WVD = B1, B1 + 16384, B1 + 32768
        O_XS = B1 + 49152
        O_XB = B1 + 65536
        O_XNT = B1 + 98304
        O_QST, O_KST, O_VST = B1 + 114688, B1 + 118784, B1 + 122880
        O_CH = B1 + 126976
        assert O_CH + 8192 <= 2 * NA

        wqd_r = bf(O_WQD, 16 * 512).rearrange("p (dc f) -> p dc f", dc=16)
        wkd_r = bf(O_WKD, 16 * 512).rearrange("p (dc f) -> p dc f", dc=16)
        wvd_r = bf(O_WVD, 16 * 512).rearrange("p (dc f) -> p dc f", dc=16)
        xq_r = bf(O_XQ, 16 * 2048).rearrange("p (dc t) -> p dc t", dc=16)
        xs_r = bf(O_XS, 16 * 512).rearrange("p (dc t) -> p dc t", dc=16)

        c_mult = 0
        for wi, (codes, sces, dsto) in enumerate(
                ((wq_c, wq_se, O_WQD), (wk_c, wk_se, O_WKD),
                 (wv_c, wv_se, O_WVD))):
            for ch in range(8):
                cod_t = f32(O_CH, 1024)
                sce_t = f32(O_CH + 4096, 1024)
                src_c = codes.rearrange("(dc p) f -> p dc f", p=P)[:, 2 * ch:2 * (ch + 1), :]
                src_s = sces.rearrange("(dc p) f -> p dc f", p=P)[:, 2 * ch:2 * (ch + 1), :]
                wait("sync", "dve", c_mult)
                do("sync", lambda x, a=cod_t, b=src_c: x.dma_start(out=a, in_=b), "wdq")
                c2 = do("sync", lambda x, a=sce_t, b=src_s: x.dma_start(out=a, in_=b), "wdq")
                dsl = bf(dsto + ch * 2048, 1024)
                wait("vector", "wdq", c2)
                c_mult = do("vector", lambda x, a=dsl, b=cod_t, c=sce_t:
                            x.tensor_mul(a, b, c), "dve")
        c_wqkv = c_mult

        O_BIAS = O_CH
        bq_sb = f32(O_BIAS, 4)
        bk_sb = f32(O_BIAS + 64, 4)
        bv_sb = f32(O_BIAS + 128, FSL)
        wait("sync", "dve", c_mult)
        do("sync", lambda x: x.dma_start(out=bq_sb, in_=bq_t), "wdq")
        do("sync", lambda x: x.dma_start(out=bk_sb, in_=bk_t), "wdq")
        c_bqkv = do("sync", lambda x: x.dma_start(out=bv_sb, in_=bv_b), "wdq")

        c_norm = [0, 0, 0, 0]
        c_chain = [0, 0, 0, 0]
        c_xnt_free = [0, 0, 0, 0]
        c_psbf_evac = [0, 0]
        c_qkv_done = 0
        c_stage_evac = {"q": 0, "k": 0, "v": 0}
        c_stage_dma = {"q": 0, "k": 0, "v": 0}
        c_xq_done = 0
        psf = (px2bf[:, 0:1024], px2bf[:, 1024:2048])
        c_xs = 0

        def emit_ln(li, j):
            main = li < NT
            iq = li - NT
            xbuf = f32(O_XB + j * 8192, 2048)
            xnt = bf(O_XNT + j * 4096, 2048)
            sd = stat[j]
            if main:
                src = x_b[li * 512 + j * P: li * 512 + (j + 1) * P, :]
            else:
                src = x_q[iq * 512 + j * P: iq * 512 + (j + 1) * P, :]
            wait("sync", "dve", c_norm[j])
            c_x = do("sync", lambda x, a=xbuf, b=src: x.dma_start(out=a, in_=b),
                     "x%d" % j)
            wait("vector", "x%d" % j, c_x)
            wait("vector", "pe", c_xnt_free[j])
            wait("vector", "dve", c_norm[j])
            c_s1 = do("vector", lambda x, o=xnt, i=xbuf, a=sd["ssq"]:
                      x.scalar_tensor_tensor(o, i, 1.0, i, ALU.mult, ALU.mult,
                                             accum_out=a), "dve")
            wait("vector", "dve", c_s1)
            c_s2 = do("vector", lambda x, o=xnt, i=xbuf, a=sd["sum"]:
                      x.tensor_scalar(o, i, 1.0, None, ALU.mult,
                                      op1=ALU.add, accum_out=a), "dve")
            wait("gpsimd", "dve", c_s2)
            do("gpsimd", lambda x, o=sd["m"], a=sd["sum"]:
               x.tensor_mul(o, a, invD), "pool")
            c2 = do("gpsimd", lambda x, o=sd["vart"], a=sd["ssq"]:
                    x.tensor_mul(o, a, invD), "pool")
            wait("gpsimd", "pool", c2)
            c3 = do("gpsimd", lambda x, o=sd["msq"], a=sd["m"]:
                    x.tensor_mul(o, a, a), "pool")
            wait("gpsimd", "pool", c3)
            c4 = do("gpsimd", lambda x, o=sd["var2"], a=sd["vart"], b=sd["msq"]:
                    x.tensor_sub(o, a, b), "pool")
            wait("scalar", "pool", c4)
            c5 = do("scalar", lambda x, o=sd["var2"], i=sd["var2"]:
                    x.activation(o, i, AF.Ln, bias=eps, scale=1.0), "act")
            wait("scalar", "act", c5)
            c6 = do("scalar", lambda x, o=sd["rstd"], i=sd["var2"]:
                    x.activation(o, i, AF.Exp, scale=-0.5), "act")
            wait("gpsimd", "act", c6)
            c_chain[j] = do("gpsimd", lambda x, o=sd["mr"], a=sd["m"],
                            b=sd["rstd"]: x.tensor_mul(o, a, b), "pool")

        def emit_norm(li, j):
            xbuf = f32(O_XB + j * 8192, 2048)
            xnt = bf(O_XNT + j * 4096, 2048)
            sd = stat[j]
            wait("vector", "pool", c_chain[j])
            c_norm[j] = do("vector", lambda x, o=xnt, i=xbuf, sd=sd:
                           x.tensor_scalar(o, i, sd["rstd"], sd["mr"],
                                           ALU.mult, ALU.subtract), "dve")

        def emit_transposes(li, j):
            nonlocal c_xq_done, c_xs
            main = li < NT
            iq = li - NT
            sl = j % 2
            xnt = bf(O_XNT + j * 4096, 2048)
            for hlf in range(2):
                pp = psf[hlf]
                wait("tensor", "dve", c_norm[j])
                if li == 0 and j == 0 and hlf == 0:
                    wait("tensor", "q0", c_eye)
                wait("tensor", "dve", c_psbf_evac[hlf])
                cpe = None
                for dcl in range(8):
                    dc = hlf * 8 + dcl
                    cpe = do("tensor",
                             lambda x, o=pp[:, dcl * P:(dcl + 1) * P],
                             i=xnt[:, dc * P:(dc + 1) * P]:
                             x.transpose(o, i, eye),
                             "pe" if dcl == 7 else None)
                if hlf == 1:
                    c_xnt_free[j] = cpe
                wait("vector", "pe", cpe)
                if main:
                    dst = xs_r[:, hlf * 8:(hlf + 1) * 8, j * P:(j + 1) * P]
                    if j == 0 and hlf == 0:
                        wait("vector", "pe", c_qkv_done)
                else:
                    dst = xq_r[:, hlf * 8:(hlf + 1) * 8,
                               iq * 512 + j * P: iq * 512 + (j + 1) * P]
                c_psbf_evac[hlf] = do(
                    "vector", lambda x, o=dst, pp=pp:
                    x.tensor_copy(o, pp.rearrange(
                        "p (dc t) -> p dc t", dc=8)), "dve")
                if not main and li == NT + NQ - 1 and j == 3 and hlf == 1:
                    c_xq_done = c_psbf_evac[hlf]
            if main and j == 3:
                c_xs = c_psbf_evac[1]

        def emit_proj(qi, name):
            nonlocal c_qkv_done
            wd_r, stg_off, bias_sb, slab = {
                "q": (wqd_r, O_QST, bq_sb, q_slab),
                "k": (wkd_r, O_KST, bk_sb, k_slab)}[name]
            stg = bf(stg_off, 4 * 512)
            for fc in range(4):
                ps = (px0, px1)[fc % 2]
                hfo = (fc // 2) * 512
                pv = ps[:, hfo:hfo + 512]
                key = (id(ps), fc // 2)
                if fc == 0:
                    wait("tensor", "dve", max(c_xs, c_wqkv))
                bank_wait("tensor", key)
                cm = None
                for dc in range(16):
                    cm = do("tensor", lambda x, o=pv,
                            l=wd_r[:, dc, fc * P:(fc + 1) * P],
                            r=xs_r[:, dc, :], s=(dc == 0), t=(dc == 15):
                            x.matmul(o, l, r, start=s, stop=t),
                            "pe" if dc == 15 else None)
                c_qkv_done = cm
                wait("vector", "pe", cm)
                if qi == 0 and fc == 0:
                    wait("vector", "wdq", c_bqkv)
                if fc == 0:
                    wait("vector", "st_" + name, c_stage_dma[name])
                ce = do("vector",
                        lambda x, o=stg[:, fc * 512:(fc + 1) * 512], i=pv,
                        b=bias_sb[:, fc:fc + 1]:
                        x.tensor_scalar(o, i, b, None, ALU.add), "dve")
                c_stage_evac[name] = ce
                bank_set(key, "dve", ce)
            dstap = slab.rearrange("(fc p) t -> p fc t", p=P)[
                :, :, qi * 512:(qi + 1) * 512]
            wait("gpsimd", "dve", c_stage_evac[name])
            c_stage_dma[name] = do(
                "gpsimd", lambda x, o=dstap,
                i=stg.rearrange("p (fc t) -> p fc t", fc=4):
                x.dma_start(out=o, in_=i), "st_" + name)

        def emit_v(qi):
            nonlocal c_qkv_done
            vstg = bf(O_VST, 4 * 512)
            for j in range(4):
                ps = (px0, px1)[j % 2]
                hfo = (j // 2) * 512
                pv = ps[:, hfo:hfo + 512]
                key = (id(ps), j // 2)
                bank_wait("tensor", key)
                cm = None
                for dc in range(16):
                    cm = do("tensor", lambda x, o=pv,
                            l=xs_r[:, dc, j * P:(j + 1) * P],
                            r=wvd_r[:, dc, :], s=(dc == 0), t=(dc == 15):
                            x.matmul(o, l, r, start=s, stop=t),
                            "pe" if dc == 15 else None)
                c_qkv_done = cm
                wait("vector", "pe", cm)
                if qi == 0 and j == 0:
                    wait("vector", "wdq", c_bqkv)
                if j == 0:
                    wait("vector", "st_v", c_stage_dma["v"])
                ce = do("vector", lambda x, o=vstg[:, j * 512:(j + 1) * 512],
                        i=pv, b=bv_sb:
                        x.scalar_tensor_tensor(o, i, 1.0, b, ALU.mult, ALU.add),
                        "dve")
                c_stage_evac["v"] = ce
                bank_set(key, "dve", ce)
            dstap = v_slab.rearrange("(i j p) f -> p i j f", p=P, j=4)[:, qi, :, :]
            wait("gpsimd", "dve", c_stage_evac["v"])
            c_stage_dma["v"] = do(
                "gpsimd", lambda x, o=dstap,
                i=vstg.rearrange("p (j f) -> p j f", j=4):
                x.dma_start(out=o, in_=i), "st_v")

        for idx in range(NT + NQ + 1):
            li = idx
            qi = idx - 1
            if li < NT + NQ:
                emit_ln(li, 0)
                emit_ln(li, 1)
            if 0 <= qi < NT:
                emit_proj(qi, "q")
            if li < NT + NQ:
                emit_ln(li, 2)
                emit_ln(li, 3)
            if 0 <= qi < NT:
                emit_proj(qi, "k")
                emit_v(qi)
            if li < NT + NQ:
                for j in range(4):
                    emit_norm(li, j)
                for j in range(4):
                    emit_transposes(li, j)

        c_p1_stores = max(c_stage_dma.values())

        if "xq_dump" in dbg:
            xq_dump = scratch("xq_dump", (P, 16 * 2048), BF16)
            wait("gpsimd", "dve", c_xq_done)
            do("gpsimd", lambda x: x.dma_start(out=xq_dump, in_=bf(O_XQ, 16 * 2048)), "st_o")

        if nphases >= 2:
            B2 = B1
            O_KH = B2
            O_VH = B2 + 32768
            O_QT = B2 + 65536
            O_PT = B2 + 67584
            O_ATH = B2 + 75776
            O_L = B2 + 92160
            assert O_L + 16384 <= 2 * NA

            wait("sync", "st_q", ctr["st_q"].n)
            wait("sync", "st_k", ctr["st_k"].n)
            wait("sync", "st_v", ctr["st_v"].n)
            wait("sync", "dve", ctr["dve"].n)
            wait("sync", "act", ctr["act"].n)
            wait("sync", "pe", ctr["pe"].n)

            c_kv_used = [0, 0]
            c_kv_load = [0, 0]
            c_qt_used = [0, 0]
            c_qt_load = [0, 0]
            c_exp2 = [0, 0]
            c_p1_clear = max(c_psbf_evac)
            c_att = [c_p1_clear, c_p1_clear]
            c_rbe = [c_p1_clear, c_p1_clear]
            c_lacc_freed = [0, 0]
            c_lacc1_freed = [0, 0]
            c_pt_pool = [0, 0, 0, 0]
            c_attdma = 0
            first_unit = True
            psx = (px0, px1)
            pvp2 = (pv_t[:, :], px2[:, 0:512])
            pzp2 = (pz[:, :], px2[:, 512:1024])
            c_pt_dve = [0, 0, 0, 0]

            for h in range(4):
                hs = h % 2
                ksb = bf(O_KH + hs * 16384, 8192)
                vsb = bf(O_VH + hs * 16384, 8192)
                wait("sync", "pe", c_kv_used[hs])
                do("sync", lambda x, o=ksb, i=k_slab[h * P:(h + 1) * P, :]:
                   x.dma_start(out=o, in_=i), "kv0" if hs == 0 else "kv1")
                vsrc = v_slab.rearrange("(tc p) f -> p tc f", p=P)
                for part in range(4):
                    c_kv_load[hs] = do(
                        "sync", lambda x, o=bf(O_VH + hs * 16384 + part * 4096, 2048
                                               ).rearrange("p (tc f) -> p tc f", tc=16),
                        i=vsrc[:, part * 16:(part + 1) * 16, h * P:(h + 1) * P]:
                        x.dma_start(out=o, in_=i), "kv0" if hs == 0 else "kv1")
                for u in range(16):
                    us = u % 2
                    up = u % 2
                    pvp = pvp2[up]
                    pzp = pzp2[up]
                    qt = bf(O_QT + us * 1024, 512)
                    wait("sync", "pe", c_qt_used[us])
                    c_qt_load[us] = do(
                        "sync", lambda x, o=qt,
                        i=q_slab[h * P:(h + 1) * P, u * 512:(u + 1) * 512]:
                        x.dma_start(out=o, in_=i), "q0" if us == 0 else "q1")

                    c_l1 = None
                    c_pv = None
                    c_st_last = None
                    c_lacc = 0
                    c_lacc1 = 0
                    c_expg = [0] * 32
                    lacc = f32(O_L + 8192 + up * 2048, 512)
                    lacc1 = f32(178176 + up * 2048, 512)
                    for g in range(34):
                        if g < 32:
                            ps = psx[g % 2]
                            pt = bf(O_PT + (g % 4) * 2048, 1024)
                            wait("tensor", "act", c_exp2[g % 2])
                            if first_unit and g < 2:
                                wait("tensor", "dve", c_p1_clear)
                            if g == 0:
                                wait("tensor", "kv0" if hs == 0 else "kv1",
                                     c_kv_load[hs])
                                wait("tensor", "q0" if us == 0 else "q1",
                                     c_qt_load[us])
                                if first_unit:
                                    wait("tensor", "dve", c_consts)
                                    wait("scalar", "dve", c_consts)
                            for jj in range(2):
                                kc = 2 * g + jj
                                c_st_last = do(
                                    "tensor",
                                    lambda x, o=ps[:, jj * 512:(jj + 1) * 512],
                                    l=ksb[:, kc * P:(kc + 1) * P], r=qt:
                                    x.matmul(o, l, r, start=True, stop=True),
                                    "pe")
                            wait("scalar", "pe", c_st_last)
                            wait("scalar", "dve", c_pt_dve[g % 4])
                            wait("scalar", "pool", c_pt_pool[g % 4])
                            ce = do("scalar", lambda x, o=pt, i=ps[:, :]:
                                    x.activation(o, i, AF.Exp, bias=negC,
                                                 scale=INV_SQRT_HD), "act")
                            c_exp2[g % 2] = ce
                            c_expg[g] = ce
                        if g >= 2:
                            gp = g - 2
                            ptp = bf(O_PT + (gp % 4) * 2048, 1024)
                            if g >= 32:
                                wait("tensor", "act", c_expg[gp])
                            if gp == 0:
                                wait("tensor", "dve", c_att[up])
                                wait("tensor", "dve", c_rbe[up])
                            for jj in range(2):
                                kc = 2 * gp + jj
                                c_pv = do("tensor", lambda x, o=pvp,
                                          l=vsb[:, kc * P:(kc + 1) * P],
                                          r=ptp[:, jj * 512:(jj + 1) * 512],
                                          s=(gp == 0 and jj == 0),
                                          t=(gp == 31 and jj == 1):
                                          x.matmul(o, l, r, start=s, stop=t),
                                          "pe" if (gp == 31 and jj == 1) else None)
                            wait("gpsimd", "act", c_expg[gp])
                            if gp == 0:
                                wait("gpsimd", "pe", c_lacc1_freed[up])
                                clb = do("gpsimd", lambda x, o=lacc1,
                                         i=ptp[:, 0:512]:
                                         x.tensor_copy(o, i), "pool")
                            else:
                                wait("gpsimd", "pool", c_lacc1)
                                clb = do("gpsimd", lambda x, o=lacc1, a=lacc1,
                                         i=ptp[:, 0:512]:
                                         x.tensor_add(o, a, i), "pool")
                            c_lacc1 = clb
                            c_pt_pool[gp % 4] = clb
                            wait("vector", "act", c_expg[gp])
                            if gp == 0:
                                wait("vector", "pe", c_lacc_freed[up])
                                cla = do("vector", lambda x, o=lacc,
                                         i=ptp[:, 512:1024]:
                                         x.tensor_copy(o, i), "dve")
                            else:
                                wait("vector", "dve", c_lacc)
                                cla = do("vector", lambda x, o=lacc, a=lacc,
                                         i=ptp[:, 512:1024]:
                                         x.tensor_add(o, a, i), "dve")
                            c_lacc = cla
                            c_pt_dve[gp % 4] = cla
                    first_unit = False
                    c_qt_used[us] = c_st_last
                    if u == 15:
                        c_kv_used[hs] = c_pv

                    l_sb = f32(O_L + up * 2048, 512, parts=1)
                    r_sb = f32(O_L + 4096 + up * 2048, 512, parts=1)
                    rb_sb = f32(O_L + 12288 + up * 2048, 512)
                    ath = bf(O_ATH, 8192)
                    wait("tensor", "dve", c_lacc)
                    c_lmm2 = do("tensor", lambda x, o=pzp[0:1, :], l=onescol,
                                r=lacc: x.matmul(o, l, r, start=True,
                                                 stop=False), "pe")
                    c_lacc_freed[up] = c_lmm2
                    wait("tensor", "pool", c_lacc1)
                    c_lmm3 = do("tensor", lambda x, o=pzp[0:1, :], l=onescol,
                                r=lacc1: x.matmul(o, l, r, start=False,
                                                  stop=True), "pe")
                    c_lacc1_freed[up] = c_lmm3
                    wait("vector", "pe", c_lmm3)
                    cle = do("vector", lambda x, o=l_sb, i=pzp[0:1, :]:
                             x.tensor_copy(o, i), "dve")
                    wait("vector", "dve", cle)
                    crr = do("vector", lambda x, o=r_sb, i=l_sb:
                             x.reciprocal(o, i), "dve")
                    wait("tensor", "dve", crr)
                    crb = do("tensor", lambda x, o=pzp, l=ones1f, r=r_sb:
                             x.matmul(o, l, r, start=True, stop=True), "pe")
                    wait("vector", "pe", crb)
                    c_rbe[up] = do("vector", lambda x, o=rb_sb, i=pzp:
                                   x.tensor_copy(o, i), "dve")
                    wait("vector", "dve", c_rbe[up])
                    wait("vector", "pe", c_pv)
                    if u == 0:
                        wait("vector", "st_a", c_attdma)
                    c_att[up] = do(
                        "vector", lambda x, o=ath[:, u * 512:(u + 1) * 512],
                        i=pvp, b=rb_sb:
                        x.scalar_tensor_tensor(o, i, 1.0, b, ALU.mult,
                                               ALU.mult), "dve")
                wait("gpsimd", "dve", max(c_att))
                c_attdma = do("gpsimd", lambda x,
                              o=attn_l[h * P:(h + 1) * P, :],
                              i=bf(O_ATH, 8192): x.dma_start(out=o, in_=i),
                              "st_a")

            if nphases >= 3 and "nocoll" not in dbg:
                wait("gpsimd", "st_a", ctr["st_a"].n)
                c_gather = do("gpsimd", lambda x: x.collective_compute(
                    "AllGather", ALU.bypass,
                    replica_groups=[[0, 1, 2, 3], [4, 5, 6, 7]],
                    ins=[attn_l.opt()], outs=[attn_g.opt()]), "cc")
            elif nphases >= 3:
                wait("gpsimd", "st_a", ctr["st_a"].n)
                c_fakeg = do("gpsimd", lambda x: x.dma_start(
                    out=attn_g[0:FSL, :], in_=attn_l[:, :]), "st_a")

        if nphases >= 3:
            B3 = B1
            O_WUC = B3
            O_WUS = B3 + 16384
            O_WUD = B3 + 32768
            O_X2S = B3 + 40960
            O_TUP = B3 + 49152
            O_AUPB = B3 + 53248
            O_BUPL = B3 + 53760
            O_BUPT = B3 + 70144
            O_LST = B3 + 70400
            assert O_LST + 16384 <= 2 * NA

            wait("sync", "dve", ctr["dve"].n)
            wait("sync", "act", ctr["act"].n)
            wait("sync", "pe", ctr["pe"].n)
            wait("sync", "st_a", ctr["st_a"].n)

            xqr = bf(O_XQ, 16 * 2048).rearrange("p (dc t) -> p dc t", dc=16)
            aupb = bf(O_AUPB, 256)
            buplb = bf(O_BUPL, 8192, parts=16)
            bup_sb = f32(O_BUPT, 64)
            tup = bf(O_TUP, 2048, parts=16)

            stg1 = f32(O_LST, 256)
            ca = do("sync", lambda x,
                    o=f32(O_LST, 256).rearrange("p (dc r) -> p dc r", dc=16),
                    i=aup.rearrange("(dc p) r -> p dc r", p=P):
                    x.dma_start(out=o, in_=i), "wu0")
            wait("vector", "wu0", ca)
            c_aupb = do("vector", lambda x, o=aupb, i=stg1:
                        x.tensor_copy(o, i), "dve")
            stg2 = f32(O_LST, 4096, parts=16)
            c_bup = 0
            for chh in range(2):
                wait("sync", "dve", c_bup if chh else c_aupb)
                cb = do("sync", lambda x, o=stg2,
                        i=bupl[:, chh * 4096:(chh + 1) * 4096]:
                        x.dma_start(out=o, in_=i), "wu0")
                wait("vector", "wu0", cb)
                c_bup = do("vector", lambda x,
                           o=buplb[:, chh * 4096:(chh + 1) * 4096], i=stg2:
                           x.tensor_copy(o, i), "dve")
            wait("sync", "dve", c_bup)
            cbt = do("sync", lambda x, o=bup_sb, i=bup_t:
                     x.dma_start(out=o, in_=i), "wu0")

            c_tup = 0
            for qq in range(4):
                wait("tensor", "dve", c_aupb)
                bank_wait("tensor", ("pz",))
                cm = None
                for dc in range(16):
                    cm = do("tensor", lambda x, o=pz[0:16, :],
                            l=aupb[:, dc * 16:(dc + 1) * 16],
                            r=xqr[:, dc, qq * 512:(qq + 1) * 512],
                            s=(dc == 0), t=(dc == 15):
                            x.matmul(o, l, r, start=s, stop=t),
                            "pe" if dc == 15 else None)
                wait("vector", "pe", cm)
                c_tup = do("vector", lambda x,
                           o=tup[:, qq * 512:(qq + 1) * 512], i=pz[0:16, :]:
                           x.tensor_copy(o, i), "dve")
                bank_set(("pz",), "dve", c_tup)

            wait("sync", "dve", c_tup)
            wait("sync", "wu0", cbt)
            c_wu_load = [0, 0]
            c_wu_used = [0, 0]
            c_wud_used = [0, 0]
            c_x2dma = 0
            c_x2evac = [0, 0]
            c_up_mm = 0
            wu_sem = ("wu0", "wu1")
            for hc in range(64):
                slh = hc % 2
                codes_t = i32(O_WUC + slh * 8192, 2048)
                sce_t = f32(O_WUS + slh * 8192, 2048)
                wud = bf(O_WUD + slh * 4096, 2048)
                wait("sync", "dve", c_wu_used[slh])
                do("sync", lambda x, o=codes_t,
                   i=wupT_q.rearrange("(dc p) h -> p dc h", p=P)[
                       :, :, hc * P:(hc + 1) * P]:
                   x.dma_start(out=o, in_=i), wu_sem[slh])
                c_wu_load[slh] = do(
                    "sync", lambda x, o=sce_t,
                    i=wup_se.rearrange("(dc p) h -> p dc h", p=P)[
                        :, :, hc * P:(hc + 1) * P]:
                    x.dma_start(out=o, in_=i), wu_sem[slh])
                wait("vector", wu_sem[slh], c_wu_load[slh])
                wait("vector", "pe", c_wud_used[slh])
                c_wu_used[slh] = do(
                    "vector", lambda x, o=wud, a=codes_t, b=sce_t:
                    x.scalar_tensor_tensor(o, a, 7.5, b, ALU.subtract,
                                           ALU.mult), "dve")
                x2st = bf(O_X2S + slh * 4096, 2048)
                wudr = wud.rearrange("p (dc h) -> p dc h", dc=16)
                for qq in range(4):
                    ps = (px0, px1)[qq % 2]
                    hfo = (qq // 2) * 512
                    pv = ps[:, hfo:hfo + 512]
                    key = (id(ps), qq // 2)
                    if qq == 0:
                        wait("tensor", "dve", max(c_wu_used[slh], c_bup, c_tup))
                    bank_wait("tensor", key)
                    cm = None
                    for dc in range(16):
                        cm = do("tensor", lambda x, o=pv, l=wudr[:, dc, :],
                                r=xqr[:, dc, qq * 512:(qq + 1) * 512],
                                s=(dc == 0), t=False:
                                x.matmul(o, l, r, start=s, stop=t), None)
                    cm = do("tensor", lambda x, o=pv,
                            l=buplb[:, hc * P:(hc + 1) * P],
                            r=tup[:, qq * 512:(qq + 1) * 512]:
                            x.matmul(o, l, r, start=False, stop=True), "pe")
                    c_up_mm = cm
                    c_wud_used[slh] = cm
                    wait("scalar", "pe", cm)
                    if hc == 0 and qq == 0:
                        wait("scalar", "wu0", cbt)
                    if qq == 0:
                        wait("scalar", "st_x2", c_x2dma)
                    ce = do("scalar", lambda x,
                            o=x2st[:, qq * 512:(qq + 1) * 512], i=pv,
                            b=bup_sb[:, hc:hc + 1]:
                            x.activation(o, i, AF.Relu, bias=b, scale=1.0),
                            "act")
                    c_x2evac[slh] = ce
                    bank_set(key, "act", ce)
                wait("gpsimd", "act", c_x2evac[slh])
                c_x2dma = do("gpsimd", lambda x,
                             o=x2_slab[hc * P:(hc + 1) * P, :], i=x2st:
                             x.dma_start(out=o, in_=i), "st_x2")

            O_Y2A = 4096
            O_X2B = 4096 + 65536
            B3d = O_X2B + 65536
            O_WDC = B3d
            O_WDS = B3d + 16384
            O_WDD = B3d + 32768
            O_YST = B3d + 49152
            O_T2 = B3d + 53248
            O_ADNB = B3d + 57344
            O_BDNL = B3d + 59392
            assert O_BDNL + 4096 <= 2 * NA

            wait("sync", "dve", ctr["dve"].n)
            wait("sync", "act", ctr["act"].n)
            wait("sync", "pe", ctr["pe"].n)
            wait("sync", "st_x2", ctr["st_x2"].n)

            t2 = bf(O_T2, 2048, parts=16)
            adnb = bf(O_ADNB, 1024)
            bdnlb = bf(O_BDNL, 2048, parts=16)

            stga = f32(O_WDC, 1024)
            cda = do("sync", lambda x,
                     o=f32(O_WDC, 1024).rearrange("p (hc r) -> p hc r", hc=64),
                     i=adn.rearrange("(hc p) r -> p hc r", p=P):
                     x.dma_start(out=o, in_=i), "wd0")
            wait("vector", "wd0", cda)
            c_adnb = do("vector", lambda x, o=adnb, i=stga:
                        x.tensor_copy(o, i), "dve")
            stgb = f32(O_WDC, 2048, parts=16)
            wait("sync", "dve", c_adnb)
            cdb = do("sync", lambda x, o=stgb, i=bdnl[:, :]:
                     x.dma_start(out=o, in_=i), "wd0")
            wait("vector", "wd0", cdb)
            c_bdnlb = do("vector", lambda x, o=bdnlb, i=stgb:
                         x.tensor_copy(o, i), "dve")

            t2ps = (pz[0:16, :], pv_t[0:16, :],
                    px2[0:16, 0:512], px2[0:16, 512:1024])
            t2keys = (("pz",), ("pv_t",), (id(px2), 0), (id(px2), 1))
            c_x2b_used = [0, 0]
            c_x2b_load = [0, 0]
            c_wd_load = [0, 0]
            c_wd_used = [0, 0]
            c_wdd_used = [0, 0]
            c_y2add = 0
            c_ydma = 0
            wd_sem = ("wd0", "wd1")
            ch_ct = 0
            c_t2e = 0
            blk = 0
            for dh in range(2):
                for hb in range(8):
                    xsl = blk % 2
                    blk += 1
                    x2b = bf(O_X2B + xsl * 32768, 8 * 2048)
                    wait("sync", "pe", c_x2b_used[xsl])
                    if dh == 0 and hb == 0:
                        wait("sync", "dve", c_bdnlb)
                    c_x2b_load[xsl] = do(
                        "sync", lambda x,
                        o=x2b.rearrange("p (hc q) -> p hc q", hc=8),
                        i=x2_slab.rearrange("(hb hc p) q -> p hb hc q", p=P,
                                            hc=8)[:, hb, :, :]:
                        x.dma_start(out=o, in_=i), "x2b" if xsl == 0 else "x2c")
                    for chk in range(4):
                        slw = ch_ct % 2
                        ch_ct += 1
                        cod = i32(O_WDC + slw * 8192, 2048)
                        sce = f32(O_WDS + slw * 8192, 2048)
                        wait("sync", "dve", c_wd_used[slw])
                        do("sync", lambda x, o=cod,
                           i=wdnT_q.rearrange("(hb hc p) d -> p hb hc d",
                                              p=P, hc=8)[
                               :, hb, 2 * chk:2 * (chk + 1),
                               dh * 1024:(dh + 1) * 1024]:
                           x.dma_start(out=o, in_=i), wd_sem[slw])
                        c_wd_load[slw] = do(
                            "sync", lambda x, o=sce,
                            i=wdn_se.rearrange("(hb hc p) d -> p hb hc d",
                                               p=P, hc=8)[
                                :, hb, 2 * chk:2 * (chk + 1),
                                dh * 1024:(dh + 1) * 1024]:
                            x.dma_start(out=o, in_=i), wd_sem[slw])
                        wait("vector", wd_sem[slw], c_wd_load[slw])
                        wait("vector", "pe", c_wdd_used[1] if ch_ct > 4 else 0)
                        c_wd_used[slw] = do(
                            "vector", lambda x,
                            o=bf(O_WDD + chk * 4096, 2048),
                            a=cod, b=sce:
                            x.scalar_tensor_tensor(o, a, 7.5, b, ALU.subtract,
                                                   ALU.mult), "dve")
                    c_wdeq = c_wd_used[(ch_ct - 1) % 2]

                    x2br = x2b.rearrange("p (hc q) -> p hc q", hc=8)
                    wddr = bf(O_WDD, 8 * 1024).rearrange("p (hc d) -> p hc d",
                                                         hc=8)
                    if dh == 0:
                        wait("tensor", "x2b" if xsl == 0 else "x2c",
                             c_x2b_load[xsl])
                        wait("tensor", "dve", c_adnb)
                        if hb == 0:
                            for kk in t2keys:
                                bank_wait("tensor", kk)
                        for hc2 in range(8):
                            hcg = hb * 8 + hc2
                            for qq in range(4):
                                cm = do("tensor", lambda x, o=t2ps[qq],
                                        l=adnb[:, hcg * 16:(hcg + 1) * 16],
                                        r=x2br[:, hc2, qq * 512:(qq + 1) * 512],
                                        s=(hb == 0 and hc2 == 0),
                                        t=(hb == 7 and hc2 == 7):
                                        x.matmul(o, l, r, start=s, stop=t),
                                        "pe" if (hb == 7 and hc2 == 7 and qq == 3)
                                        else None)
                        if hb == 7:
                            c_t2mm = cm
                    wait("tensor", "x2b" if xsl == 0 else "x2c",
                         c_x2b_load[xsl])
                    wait("tensor", "dve", c_wdeq)
                    cm = None
                    for tc in range(16):
                        for d5 in range(2):
                            ps = (px0, px1)[(tc * 2 + d5) % 2]
                            hfo = (((tc * 2 + d5) // 2) % 2) * 512
                            pv = ps[:, hfo:hfo + 512]
                            key = (id(ps), hfo // 512)
                            bank_wait("tensor", key)
                            for hc2 in range(8):
                                cm = do("tensor", lambda x, o=pv,
                                        l=x2br[:, hc2, tc * P:(tc + 1) * P],
                                        r=wddr[:, hc2, d5 * 512:(d5 + 1) * 512],
                                        s=(hc2 == 0), t=(hc2 == 7):
                                        x.matmul(o, l, r, start=s, stop=t),
                                        "pe" if hc2 == 7 else None)
                            wait("vector", "pe", cm)
                            dsl = f32(O_Y2A + tc * 4096 + d5 * 2048, 512)
                            if hb == 0:
                                cy = do("vector", lambda x, o=dsl, i=pv:
                                        x.tensor_copy(o, i), "dve")
                            else:
                                cy = do("vector", lambda x, o=dsl, a=dsl, i=pv:
                                        x.tensor_add(o, a, i), "dve")
                            c_y2add = cy
                            bank_set(key, "dve", cy)
                    c_x2b_used[xsl] = cm
                    c_wdd_used[1] = cm

                if dh == 0:
                    wait("vector", "pe", c_t2mm)
                    for qq in range(4):
                        c_t2e = do("vector", lambda x,
                                   o=t2[:, qq * 512:(qq + 1) * 512],
                                   i=t2ps[qq]: x.tensor_copy(o, i), "dve")
                        bank_set(t2keys[qq], "dve", c_t2e)

                for tc in range(16):
                    for d5 in range(2):
                        ps = (px0, px1)[(tc * 2 + d5) % 2]
                        hfo = (((tc * 2 + d5) // 2) % 2) * 512
                        pv = ps[:, hfo:hfo + 512]
                        key = (id(ps), hfo // 512)
                        bank_wait("tensor", key)
                        if tc == 0 and d5 == 0:
                            wait("tensor", "dve", max(c_t2e, c_bdnlb))
                        cm = do("tensor", lambda x, o=pv,
                                l=t2[:, tc * P:(tc + 1) * P],
                                r=bdnlb[:, dh * 1024 + d5 * 512:
                                        dh * 1024 + (d5 + 1) * 512]:
                                x.matmul(o, l, r, start=True, stop=True), "pe")
                        wait("vector", "pe", cm)
                        dsl = f32(O_Y2A + tc * 4096 + d5 * 2048, 512)
                        cy = do("vector", lambda x, o=dsl, a=dsl, i=pv:
                                x.tensor_add(o, a, i), "dve")
                        bank_set(key, "dve", cy)
                    ys = bf(O_YST + (tc % 2) * 2048, 1024)
                    wait("vector", "dve", cy)
                    wait("vector", "st_y2", c_ydma)
                    cyv = do("vector", lambda x, o=ys,
                             i=f32(O_Y2A + tc * 4096, 1024):
                             x.tensor_copy(o, i), "dve")
                    wait("gpsimd", "dve", cyv)
                    c_ydma = do("gpsimd", lambda x,
                                o=y2_slab[tc * P:(tc + 1) * P,
                                          dh * 1024:(dh + 1) * 1024], i=ys:
                                x.dma_start(out=o, in_=i), "st_y2")

        if nphases >= 4:
            B4 = 2048
            O_WOD = B4
            O_WOC = B4 + 65536
            O_WOS = B4 + 81920
            O_AG = B4 + 98304
            O_XR = B4 + 106496
            O_YR = B4 + 122880
            O_OST = B4 + 131072
            O_XY = B4 + 147456
            O_BO = B4 + 163840
            assert O_BO + 24576 <= 2 * NA

            wait("sync", "dve", ctr["dve"].n)
            wait("sync", "act", ctr["act"].n)
            wait("sync", "pe", ctr["pe"].n)
            wait("sync", "st_y2", ctr["st_y2"].n)
            if "nocoll" not in dbg:
                wait("sync", "cc", 1)
            else:
                wait("sync", "st_a", ctr["st_a"].n)

            wod = bf(O_WOD, 16 * 2048)
            bo_sb = f32(O_BO, 2048)
            bdn_sb = f32(O_BO + 8192, 2048)
            bb = f32(O_BO + 16384, 2048)
            do("sync", lambda x: x.dma_start(out=bo_sb, in_=bo_b), "wo_s")
            cbb = do("sync", lambda x: x.dma_start(out=bdn_sb, in_=bdn_b), "wo_s")
            wait("vector", "wo_s", cbb)
            c_bb = do("vector", lambda x: x.tensor_add(bb, bo_sb, bdn_sb), "dve")
            wait("sync", "dve", c_bb)

            c_wo_load = [0, 0]
            c_wo_used = [0, 0]
            c_wo_mult = 0
            wo_sem = ("wu0", "wu1")
            for dc in range(16):
                slw = dc % 2
                codt = f32(O_WOC + slw * 8192, 2048)
                scet = f32(O_WOS + slw * 8192, 2048)
                wait("sync", "dve", c_wo_used[slw])
                do("sync", lambda x, o=codt,
                   i=wo_c.rearrange("(dc p) d -> p dc d", p=P)[:, dc, :]:
                   x.dma_start(out=o, in_=i), wo_sem[slw])
                c_wo_load[slw] = do(
                    "sync", lambda x, o=scet,
                    i=wo_se.rearrange("(dc p) d -> p dc d", p=P)[:, dc, :]:
                    x.dma_start(out=o, in_=i), wo_sem[slw])
                wait("vector", wo_sem[slw], c_wo_load[slw])
                c_wo_mult = do("vector", lambda x,
                               o=wod[:, dc * 2048:(dc + 1) * 2048],
                               a=codt, b=scet: x.tensor_mul(o, a, b), "dve")
                c_wo_used[slw] = c_wo_mult

            wodr = wod.rearrange("p (dc d) -> p dc d", dc=16)
            qoff_sb = i32(O_BO + 24576, 1, parts=1)
            cqo = do("sync", lambda x: x.dma_start(out=qoff_sb, in_=qoff), "wo_s")
            wait("sync", "wo_s", cqo)
            do("sync", lambda x: x.reg_load(REGS["rb"], qoff_sb))
            agt = attn_g.tensor
            c_pa_load = [0, 0]
            c_pa_used = [0, 0]
            c_xy = [0, 0]
            c_ost = [0, 0]
            c_odma = 0
            for tk in range(16):
                slp = tk % 2
                ag = bf(O_AG + slp * 4096, 2048)
                xr = f32(O_XR + slp * 8192, 2048)
                yr = bf(O_YR + slp * 4096, 2048)
                xy = f32(O_XY + slp * 8192, 2048)
                ost = f32(O_OST + slp * 8192, 2048)
                pa = "pa0" if slp == 0 else "pa1"
                wait("sync", "pe", c_pa_used[slp])
                wait("sync", "dve", c_xy[slp])
                do("sync", lambda x, t=tk:
                   x.reg_alu(REGS["ra"], REGS["rb"], t * P, ALU.add))
                do("sync", lambda x, o=ag.rearrange("p (fc t) -> p fc t", fc=16):
                   x.dma_start(out=o, in_=bass.AP(
                       agt, REGS["ra"], [[S, P], [P * S, 16], [1, P]])), pa)
                do("sync", lambda x, o=xr, i=x_q[tk * P:(tk + 1) * P, :]:
                   x.dma_start(out=o, in_=i), pa)
                c_pa_load[slp] = do(
                    "sync", lambda x, o=yr, i=y2_slab[tk * P:(tk + 1) * P, :]:
                    x.dma_start(out=o, in_=i), pa)

                wait("vector", pa, c_pa_load[slp])
                wait("vector", "dve", c_bb)
                cx1 = do("vector", lambda x, o=xy, a=xr, b=yr:
                         x.tensor_add(o, a, b), "dve")
                wait("vector", "dve", cx1)
                cx2 = do("vector", lambda x, o=xy, a=xy, b=bb:
                         x.tensor_add(o, a, b), "dve")
                c_xy[slp] = cx2

                cm_last = 0
                for d5 in range(4):
                    ps = (px0, px1)[d5 % 2]
                    hfo = (d5 // 2) * 512
                    pv = ps[:, hfo:hfo + 512]
                    key = (id(ps), hfo // 512)
                    if tk == 0 and d5 == 0:
                        wait("tensor", "dve", c_wo_mult)
                    if d5 == 0:
                        wait("tensor", pa, c_pa_load[slp])
                    bank_wait("tensor", key)
                    cm = None
                    for fc in range(16):
                        cm = do("tensor", lambda x, o=pv,
                                l=ag[:, fc * P:(fc + 1) * P],
                                r=wodr[:, fc, d5 * 512:(d5 + 1) * 512],
                                s=(fc == 0), t=(fc == 15):
                                x.matmul(o, l, r, start=s, stop=t),
                                "pe" if fc == 15 else None)
                    cm_last = cm
                    wait("vector", "pe", cm)
                    wait("vector", "dve", c_xy[slp])
                    if tk >= 2 and d5 == 0:
                        wait("vector", "st_o", c_odma)
                    co = do("vector", lambda x,
                            o=ost[:, d5 * 512:(d5 + 1) * 512], i=pv,
                            b=xy[:, d5 * 512:(d5 + 1) * 512]:
                            x.scalar_tensor_tensor(o, i, 1.0, b, ALU.mult,
                                                   ALU.add), "dve")
                    c_ost[slp] = co
                    bank_set(key, "dve", co)
                c_pa_used[slp] = cm_last
                wait("gpsimd", "dve", c_ost[slp])
                c_odma = do("gpsimd", lambda x,
                            o=out[tk * P:(tk + 1) * P, :], i=ost:
                            x.dma_start(out=o, in_=i), "st_o")

        for snm in DMA_SEMS:
            if snm.startswith("st_") and ctr[snm].n:
                wait("gpsimd", snm, ctr[snm].n)

        block = ctx.enter_context(nc.Block())
        REGS = {}

        @block.sync
        def _(sync):
            with sync.register("rb") as rb, sync.register("ra") as ra:
                REGS["rb"] = rb
                REGS["ra"] = ra
                for f in ops["sync"]:
                    f(sync)

        @block.tensor
        def _(tensor):
            for f in ops["tensor"]:
                f(tensor)

        @block.vector
        def _(vector):
            for f in ops["vector"]:
                f(vector)

        @block.scalar
        def _(scalar):
            for f in ops["scalar"]:
                f(scalar)

        @block.gpsimd
        def _(gpsimd):
            for f in ops["gpsimd"]:
                f(gpsimd)

    return nc



def _prep_inputs(inputs):
    x = inputs["L0_attn_x"]
    g = {}
    g["wo_c"] = np.ascontiguousarray(inputs["L0_attn_wo_codes"])
    g["wo_se"] = np.repeat(inputs["L0_attn_wo_scales"], 32, axis=0)
    g["bo_b"] = np.broadcast_to(inputs["L0_attn_bo"], (P, D)).copy()
    g["bdn_b"] = np.broadcast_to(inputs["L0_ffn_b_down"], (P, D)).copy()
    g["bup_t"] = np.ascontiguousarray(inputs["L0_ffn_b_up"].reshape(64, P).T)
    g["wupT_q"] = np.ascontiguousarray(inputs["L0_ffn_w_up_q"].T)
    g["wup_se"] = np.ascontiguousarray(
        np.repeat(inputs["L0_ffn_w_up_scale"], 64, axis=1).T)
    g["wdnT_q"] = np.ascontiguousarray(inputs["L0_ffn_w_down_q"].T)
    g["wdn_se"] = np.ascontiguousarray(
        np.repeat(inputs["L0_ffn_w_down_scale"], 64, axis=1).T)
    g["aup"] = inputs["L0_ffn_w_up_lora_a"]
    g["bupl"] = inputs["L0_ffn_w_up_lora_b"]
    g["adn"] = inputs["L0_ffn_w_down_lora_a"]
    g["bdnl"] = inputs["L0_ffn_w_down_lora_b"]
    g["eye_bf"] = np.eye(P, dtype=ml_dtypes.bfloat16)

    in_maps = []
    for c in range(NCORE):
        b, r = c // 4, c % 4
        sl = slice(r * FSL, (r + 1) * FSL)
        m = dict(g)
        m["x_b"] = x[b]
        m["x_q"] = np.ascontiguousarray(x[b, r * QTOK:(r + 1) * QTOK, :])
        for nm in ("q", "k", "v"):
            m[f"w{nm}_c"] = np.ascontiguousarray(
                inputs[f"L0_attn_w{nm}_codes"][:, sl])
            m[f"w{nm}_se"] = np.repeat(
                inputs[f"L0_attn_w{nm}_scales"][:, sl], 32, axis=0)
        m["bq_t"] = np.ascontiguousarray(inputs["L0_attn_bq"][sl].reshape(4, P).T)
        m["bk_t"] = np.ascontiguousarray(inputs["L0_attn_bk"][sl].reshape(4, P).T)
        m["bv_b"] = np.broadcast_to(inputs["L0_attn_bv"][sl], (P, FSL)).copy()
        m["qoff"] = np.array([[r * QTOK]], dtype=np.int32)
        in_maps.append(m)
    return in_maps


def kernel(**inputs):
    in_maps = _prep_inputs(inputs)
    nc = build_program()
    res = run_bass_kernel_spmd(nc, in_maps, core_ids=list(range(NCORE)))
    out = np.empty((2, S, D), dtype=np.float32)
    for c in range(NCORE):
        b, r = c // 4, c % 4
        out[b, r * QTOK:(r + 1) * QTOK, :] = res.results[c]["out"]
    return out


# version 15
# speedup vs baseline: 1.0777x; 1.0777x over previous
import numpy as np
import ml_dtypes
from contextlib import ExitStack

import concourse.bass as bass
import concourse.mybir as mybir
from concourse.bass_utils import run_bass_kernel_spmd

F32 = mybir.dt.float32
BF16 = mybir.dt.bfloat16
I32 = mybir.dt.int32
AF = mybir.ActivationFunctionType
ALU = mybir.AluOpType

P = 128
S = 8192
D = 2048
H = 8192
NCORE = 8
FSL = 512
NT = 16
NQ = 4
QTOK = S // 4
CSHIFT = 16.0
INV_SQRT_HD = 0.08838834764831845

ENGS = ("sync", "tensor", "vector", "scalar", "gpsimd")


class Ctr:
    __slots__ = ("n",)

    def __init__(self):
        self.n = 0

    def bump(self, k):
        self.n += k
        return self.n


ENG_SEMS = ("pe", "act", "dve", "pool", "cc")
DMA_SEMS = ("x0", "x1", "x2", "x3", "wdq", "kv0", "kv1", "q0", "q1", "wu0", "wu1",
            "x2b", "x2c", "wd0", "wd1", "pa0", "pa1", "wo_s",
            "st_q", "st_k", "st_v", "st_a", "st_x2", "st_y2", "st_o")
ALL_SEMS = ENG_SEMS + DMA_SEMS


def build_program(nphases=4, dbg=()):
    nc = bass.Bass("TRN2", target_bir_lowering=False)
    ops = {e: [] for e in ENGS}
    ctr = {s: Ctr() for s in ALL_SEMS}
    H_ = {}

    def wait(eng, sem, val):
        if val and val > 0:
            ops[eng].append(lambda e, s=sem, v=val: e.wait_ge(H_[s], v))

    def do(eng, fn, inc=None):
        if inc is None:
            ops[eng].append(fn)
            return None
        amt = 1 if inc in ENG_SEMS else 16
        c = ctr[inc].bump(amt)
        ops[eng].append(lambda e, f=fn, s=inc, a=amt: f(e).then_inc(H_[s], a))
        return c

    def inp(name, shape, dt):
        return nc.dram_tensor(name, list(shape), dt, kind="ExternalInput").ap()

    x_b = inp("x_b", (S, D), F32)
    x_q = inp("x_q", (QTOK, D), F32)
    wq_c = inp("wq_c", (D, FSL), F32)
    wk_c = inp("wk_c", (D, FSL), F32)
    wv_c = inp("wv_c", (D, FSL), F32)
    wq_se = inp("wq_se", (D, FSL), F32)
    wk_se = inp("wk_se", (D, FSL), F32)
    wv_se = inp("wv_se", (D, FSL), F32)
    wo_c = inp("wo_c", (D, D), F32)
    wo_se = inp("wo_se", (D, D), F32)
    bq_t = inp("bq_t", (P, 4), F32)
    bk_t = inp("bk_t", (P, 4), F32)
    bv_b = inp("bv_b", (P, FSL), F32)
    bo_b = inp("bo_b", (P, D), F32)
    bdn_b = inp("bdn_b", (P, D), F32)
    bup_t = inp("bup_t", (P, 64), F32)
    wupT_q = inp("wupT_q", (D, H), I32)
    wup_se = inp("wup_se", (D, H), F32)
    wdnT_q = inp("wdnT_q", (H, D), I32)
    wdn_se = inp("wdn_se", (H, D), F32)
    aup = inp("aup", (D, 16), F32)
    bupl = inp("bupl", (16, H), F32)
    adn = inp("adn", (H, 16), F32)
    bdnl = inp("bdnl", (16, D), F32)
    eye_in = inp("eye_bf", (P, P), BF16)
    qoff = inp("qoff", (1, 1), I32)

    out = nc.dram_tensor("out", [QTOK, D], F32, kind="ExternalOutput").ap()

    def scratch(name, shape, dt):
        kind = "ExternalOutput" if name in dbg else "Internal"
        return nc.dram_tensor(name, list(shape), dt, kind=kind).ap()

    q_slab = scratch("q_slab", (FSL, S), BF16)
    k_slab = scratch("k_slab", (FSL, S), BF16)
    v_slab = scratch("v_slab", (S, FSL), BF16)
    attn_l = scratch("attn_l", (FSL, S), BF16)
    attn_g = scratch("attn_g", (4 * FSL, S), BF16)
    x2_slab = scratch("x2_slab", (H, D), BF16)
    y2_slab = scratch("y2_slab", (QTOK, D), BF16)

    with ExitStack() as ctx:
        NA = 102400
        arena = ctx.enter_context(nc.sbuf_tensor("arena", [P, NA], BF16))
        arena_ap = arena[:, :]
        arena_f = arena[:, :].bitcast(F32)
        arena_i = arena[:, :].bitcast(I32)

        def bf(off_b, n, parts=P):
            assert off_b % 2 == 0 and off_b + 2 * n <= 2 * NA
            return arena_ap[0:parts, off_b // 2: off_b // 2 + n]

        def f32(off_b, n, parts=P):
            assert off_b % 4 == 0 and off_b + 4 * n <= 2 * NA
            return arena_f[0:parts, off_b // 4: off_b // 4 + n]

        def i32(off_b, n, parts=P):
            assert off_b % 4 == 0 and off_b + 4 * n <= 2 * NA
            return arena_i[0:parts, off_b // 4: off_b // 4 + n]

        px0 = ctx.enter_context(nc.psum_tensor("px0", [P, 1024], F32))
        px1 = ctx.enter_context(nc.psum_tensor("px1", [P, 1024], F32))
        px2 = ctx.enter_context(nc.psum_tensor("px2", [P, 1024], F32))
        pv_t = ctx.enter_context(nc.psum_tensor("pv_t", [P, 512], F32))
        pz = ctx.enter_context(nc.psum_tensor("pz", [P, 512], F32))
        px2bf = px2[:, :].bitcast(BF16)
        for s in ALL_SEMS:
            H_[s] = ctx.enter_context(nc.semaphore(s))

        bank_free = {}

        def bank_wait(eng, key):
            sc = bank_free.get(key)
            if sc:
                wait(eng, sc[0], sc[1])

        def bank_set(key, sem, count):
            bank_free[key] = (sem, count)

        O_SMALL = 0
        eye = bf(0, P)
        o = 256
        ones_bf = bf(o, 1); o += 64
        negC = f32(o, 1); o += 64
        eps = f32(o, 1); o += 64
        ones1f = f32(o, 128, parts=1); o += 512
        onescol = f32(o, 1); o += 64
        invD = f32(o, 1); o += 64
        stat = []
        for j in range(4):
            d = {}
            for nm in ("sum", "ssq", "m", "vart", "msq", "var2", "rstd", "mr"):
                d[nm] = f32(o, 1); o += 64
            stat.append(d)
        assert o <= 4096

        c_eye = do("sync", lambda x: x.dma_start(out=eye, in_=eye_in), "q0")
        do("vector", lambda x: x.memset(ones_bf, 1.0), "dve")
        do("vector", lambda x: x.memset(negC, -CSHIFT), "dve")
        do("vector", lambda x: x.memset(eps, 1e-5), "dve")
        do("vector", lambda x: x.memset(ones1f, 1.0), "dve")
        do("vector", lambda x: x.memset(onescol, 1.0), "dve")
        c_consts = do("vector", lambda x: x.memset(invD, 1.0 / D), "dve")

        O_XQ = 4096
        B1 = O_XQ + 65536
        O_WQD, O_WKD, O_WVD = B1, B1 + 16384, B1 + 32768
        O_XS = B1 + 49152
        O_XB = B1 + 65536
        O_XNT = B1 + 98304
        O_QST, O_KST, O_VST = B1 + 114688, B1 + 118784, B1 + 122880
        O_CH = B1 + 126976
        assert O_CH + 8192 <= 2 * NA

        wqd_r = bf(O_WQD, 16 * 512).rearrange("p (dc f) -> p dc f", dc=16)
        wkd_r = bf(O_WKD, 16 * 512).rearrange("p (dc f) -> p dc f", dc=16)
        wvd_r = bf(O_WVD, 16 * 512).rearrange("p (dc f) -> p dc f", dc=16)
        xq_r = bf(O_XQ, 16 * 2048).rearrange("p (dc t) -> p dc t", dc=16)
        xs_r = bf(O_XS, 16 * 512).rearrange("p (dc t) -> p dc t", dc=16)

        c_mult = 0
        for wi, (codes, sces, dsto) in enumerate(
                ((wq_c, wq_se, O_WQD), (wk_c, wk_se, O_WKD),
                 (wv_c, wv_se, O_WVD))):
            for ch in range(8):
                cod_t = f32(O_CH, 1024)
                sce_t = f32(O_CH + 4096, 1024)
                src_c = codes.rearrange("(dc p) f -> p dc f", p=P)[:, 2 * ch:2 * (ch + 1), :]
                src_s = sces.rearrange("(dc p) f -> p dc f", p=P)[:, 2 * ch:2 * (ch + 1), :]
                wait("sync", "dve", c_mult)
                do("sync", lambda x, a=cod_t, b=src_c: x.dma_start(out=a, in_=b), "wdq")
                c2 = do("sync", lambda x, a=sce_t, b=src_s: x.dma_start(out=a, in_=b), "wdq")
                dsl = bf(dsto + ch * 2048, 1024)
                wait("vector", "wdq", c2)
                c_mult = do("vector", lambda x, a=dsl, b=cod_t, c=sce_t:
                            x.tensor_mul(a, b, c), "dve")
        c_wqkv = c_mult

        O_BIAS = O_CH
        bq_sb = f32(O_BIAS, 4)
        bk_sb = f32(O_BIAS + 64, 4)
        bv_sb = f32(O_BIAS + 128, FSL)
        wait("sync", "dve", c_mult)
        do("sync", lambda x: x.dma_start(out=bq_sb, in_=bq_t), "wdq")
        do("sync", lambda x: x.dma_start(out=bk_sb, in_=bk_t), "wdq")
        c_bqkv = do("sync", lambda x: x.dma_start(out=bv_sb, in_=bv_b), "wdq")

        c_norm = [0, 0, 0, 0]
        c_chain = [0, 0, 0, 0]
        c_xnt_free = [0, 0, 0, 0]
        c_psbf_evac = [0, 0]
        c_qkv_done = 0
        c_stage_evac = {"q": 0, "k": 0, "v": 0}
        c_stage_dma = {"q": 0, "k": 0, "v": 0}
        c_xq_done = 0
        psf = (px2bf[:, 0:1024], px2bf[:, 1024:2048])
        c_xs = 0

        def emit_ln(li, j):
            main = li < NT
            iq = li - NT
            xbuf = f32(O_XB + j * 8192, 2048)
            xnt = bf(O_XNT + j * 4096, 2048)
            sd = stat[j]
            if main:
                src = x_b[li * 512 + j * P: li * 512 + (j + 1) * P, :]
            else:
                src = x_q[iq * 512 + j * P: iq * 512 + (j + 1) * P, :]
            wait("sync", "dve", c_norm[j])
            c_x = do("sync", lambda x, a=xbuf, b=src: x.dma_start(out=a, in_=b),
                     "x%d" % j)
            wait("vector", "x%d" % j, c_x)
            wait("vector", "pe", c_xnt_free[j])
            wait("vector", "dve", c_norm[j])
            c_s1 = do("vector", lambda x, o=xnt, i=xbuf, a=sd["ssq"]:
                      x.scalar_tensor_tensor(o, i, 1.0, i, ALU.mult, ALU.mult,
                                             accum_out=a), "dve")
            wait("vector", "dve", c_s1)
            c_s2 = do("vector", lambda x, o=xnt, i=xbuf, a=sd["sum"]:
                      x.tensor_scalar(o, i, 1.0, None, ALU.mult,
                                      op1=ALU.add, accum_out=a), "dve")
            wait("gpsimd", "dve", c_s2)
            do("gpsimd", lambda x, o=sd["m"], a=sd["sum"]:
               x.tensor_mul(o, a, invD), "pool")
            c2 = do("gpsimd", lambda x, o=sd["vart"], a=sd["ssq"]:
                    x.tensor_mul(o, a, invD), "pool")
            wait("gpsimd", "pool", c2)
            c3 = do("gpsimd", lambda x, o=sd["msq"], a=sd["m"]:
                    x.tensor_mul(o, a, a), "pool")
            wait("gpsimd", "pool", c3)
            c4 = do("gpsimd", lambda x, o=sd["var2"], a=sd["vart"], b=sd["msq"]:
                    x.tensor_sub(o, a, b), "pool")
            wait("scalar", "pool", c4)
            c5 = do("scalar", lambda x, o=sd["var2"], i=sd["var2"]:
                    x.activation(o, i, AF.Ln, bias=eps, scale=1.0), "act")
            wait("scalar", "act", c5)
            c6 = do("scalar", lambda x, o=sd["rstd"], i=sd["var2"]:
                    x.activation(o, i, AF.Exp, scale=-0.5), "act")
            wait("gpsimd", "act", c6)
            c_chain[j] = do("gpsimd", lambda x, o=sd["mr"], a=sd["m"],
                            b=sd["rstd"]: x.tensor_mul(o, a, b), "pool")

        def emit_norm(li, j):
            xbuf = f32(O_XB + j * 8192, 2048)
            xnt = bf(O_XNT + j * 4096, 2048)
            sd = stat[j]
            wait("vector", "pool", c_chain[j])
            c_norm[j] = do("vector", lambda x, o=xnt, i=xbuf, sd=sd:
                           x.tensor_scalar(o, i, sd["rstd"], sd["mr"],
                                           ALU.mult, ALU.subtract), "dve")

        def emit_transposes(li, j):
            nonlocal c_xq_done, c_xs
            main = li < NT
            iq = li - NT
            sl = j % 2
            xnt = bf(O_XNT + j * 4096, 2048)
            for hlf in range(2):
                pp = psf[hlf]
                wait("tensor", "dve", c_norm[j])
                if li == 0 and j == 0 and hlf == 0:
                    wait("tensor", "q0", c_eye)
                wait("tensor", "dve", c_psbf_evac[hlf])
                cpe = None
                for dcl in range(8):
                    dc = hlf * 8 + dcl
                    cpe = do("tensor",
                             lambda x, o=pp[:, dcl * P:(dcl + 1) * P],
                             i=xnt[:, dc * P:(dc + 1) * P]:
                             x.transpose(o, i, eye),
                             "pe" if dcl == 7 else None)
                if hlf == 1:
                    c_xnt_free[j] = cpe
                wait("vector", "pe", cpe)
                if main:
                    dst = xs_r[:, hlf * 8:(hlf + 1) * 8, j * P:(j + 1) * P]
                    if j == 0 and hlf == 0:
                        wait("vector", "pe", c_qkv_done)
                else:
                    dst = xq_r[:, hlf * 8:(hlf + 1) * 8,
                               iq * 512 + j * P: iq * 512 + (j + 1) * P]
                c_psbf_evac[hlf] = do(
                    "vector", lambda x, o=dst, pp=pp:
                    x.tensor_copy(o, pp.rearrange(
                        "p (dc t) -> p dc t", dc=8)), "dve")
                if not main and li == NT + NQ - 1 and j == 3 and hlf == 1:
                    c_xq_done = c_psbf_evac[hlf]
            if main and j == 3:
                c_xs = c_psbf_evac[1]

        def emit_proj(qi, name):
            nonlocal c_qkv_done
            wd_r, stg_off, bias_sb, slab = {
                "q": (wqd_r, O_QST, bq_sb, q_slab),
                "k": (wkd_r, O_KST, bk_sb, k_slab)}[name]
            stg = bf(stg_off, 4 * 512)
            for fc in range(4):
                ps = (px0, px1)[fc % 2]
                hfo = (fc // 2) * 512
                pv = ps[:, hfo:hfo + 512]
                key = (id(ps), fc // 2)
                if fc == 0:
                    wait("tensor", "dve", max(c_xs, c_wqkv))
                bank_wait("tensor", key)
                cm = None
                for dc in range(16):
                    cm = do("tensor", lambda x, o=pv,
                            l=wd_r[:, dc, fc * P:(fc + 1) * P],
                            r=xs_r[:, dc, :], s=(dc == 0), t=(dc == 15):
                            x.matmul(o, l, r, start=s, stop=t),
                            "pe" if dc == 15 else None)
                c_qkv_done = cm
                wait("vector", "pe", cm)
                if qi == 0 and fc == 0:
                    wait("vector", "wdq", c_bqkv)
                if fc == 0:
                    wait("vector", "st_" + name, c_stage_dma[name])
                ce = do("vector",
                        lambda x, o=stg[:, fc * 512:(fc + 1) * 512], i=pv,
                        b=bias_sb[:, fc:fc + 1]:
                        x.tensor_scalar(o, i, b, None, ALU.add), "dve")
                c_stage_evac[name] = ce
                bank_set(key, "dve", ce)
            dstap = slab.rearrange("(fc p) t -> p fc t", p=P)[
                :, :, qi * 512:(qi + 1) * 512]
            wait("gpsimd", "dve", c_stage_evac[name])
            c_stage_dma[name] = do(
                "gpsimd", lambda x, o=dstap,
                i=stg.rearrange("p (fc t) -> p fc t", fc=4):
                x.dma_start(out=o, in_=i), "st_" + name)

        def emit_v(qi):
            nonlocal c_qkv_done
            vstg = bf(O_VST, 4 * 512)
            for j in range(4):
                ps = (px0, px1)[j % 2]
                hfo = (j // 2) * 512
                pv = ps[:, hfo:hfo + 512]
                key = (id(ps), j // 2)
                bank_wait("tensor", key)
                cm = None
                for dc in range(16):
                    cm = do("tensor", lambda x, o=pv,
                            l=xs_r[:, dc, j * P:(j + 1) * P],
                            r=wvd_r[:, dc, :], s=(dc == 0), t=(dc == 15):
                            x.matmul(o, l, r, start=s, stop=t),
                            "pe" if dc == 15 else None)
                c_qkv_done = cm
                wait("vector", "pe", cm)
                if qi == 0 and j == 0:
                    wait("vector", "wdq", c_bqkv)
                if j == 0:
                    wait("vector", "st_v", c_stage_dma["v"])
                ce = do("vector", lambda x, o=vstg[:, j * 512:(j + 1) * 512],
                        i=pv, b=bv_sb:
                        x.scalar_tensor_tensor(o, i, 1.0, b, ALU.mult, ALU.add),
                        "dve")
                c_stage_evac["v"] = ce
                bank_set(key, "dve", ce)
            dstap = v_slab.rearrange("(i j p) f -> p i j f", p=P, j=4)[:, qi, :, :]
            wait("gpsimd", "dve", c_stage_evac["v"])
            c_stage_dma["v"] = do(
                "gpsimd", lambda x, o=dstap,
                i=vstg.rearrange("p (j f) -> p j f", j=4):
                x.dma_start(out=o, in_=i), "st_v")

        for idx in range(NT + NQ + 1):
            li = idx
            qi = idx - 1
            if li < NT + NQ:
                emit_ln(li, 0)
                emit_ln(li, 1)
            if 0 <= qi < NT:
                emit_proj(qi, "q")
            if li < NT + NQ:
                emit_ln(li, 2)
                emit_ln(li, 3)
            if 0 <= qi < NT:
                emit_proj(qi, "k")
                emit_v(qi)
            if li < NT + NQ:
                for j in range(4):
                    emit_norm(li, j)
                for j in range(4):
                    emit_transposes(li, j)

        c_p1_stores = max(c_stage_dma.values())

        if "xq_dump" in dbg:
            xq_dump = scratch("xq_dump", (P, 16 * 2048), BF16)
            wait("gpsimd", "dve", c_xq_done)
            do("gpsimd", lambda x: x.dma_start(out=xq_dump, in_=bf(O_XQ, 16 * 2048)), "st_o")

        if nphases >= 2:
            B2 = B1
            O_KH = B2
            O_VH = B2 + 32768
            O_QT = B2 + 65536
            O_PT = B2 + 67584
            O_ATH = B2 + 75776
            O_L = B2 + 92160
            assert O_L + 16384 <= 2 * NA

            wait("sync", "st_q", ctr["st_q"].n)
            wait("sync", "st_k", ctr["st_k"].n)
            wait("sync", "st_v", ctr["st_v"].n)
            wait("sync", "dve", ctr["dve"].n)
            wait("sync", "act", ctr["act"].n)
            wait("sync", "pe", ctr["pe"].n)

            c_kv_used = [0, 0]
            c_kv_load = [0, 0]
            c_qt_used = [0, 0]
            c_qt_load = [0, 0]
            c_exp2 = [0, 0]
            c_p1_clear = max(c_psbf_evac)
            c_att = [c_p1_clear, c_p1_clear]
            c_rbe = [c_p1_clear, c_p1_clear]
            c_lacc_freed = [0, 0]
            c_attdma = 0
            first_unit = True
            psx = (px0, px1)
            pvp2 = (pv_t[:, :], px2[:, 0:512])
            pzp2 = (pz[:, :], px2[:, 512:1024])
            wait("scalar", "st_q", ctr["st_q"].n)
            wait("scalar", "st_k", ctr["st_k"].n)
            wait("scalar", "st_v", ctr["st_v"].n)

            for h in range(4):
                hs = h % 2
                ksb = bf(O_KH + hs * 16384, 8192)
                vsb = bf(O_VH + hs * 16384, 8192)
                wait("sync", "pe", c_kv_used[hs])
                do("sync", lambda x, o=ksb, i=k_slab[h * P:(h + 1) * P, :]:
                   x.dma_start(out=o, in_=i), "kv0" if hs == 0 else "kv1")
                vsrc = v_slab.rearrange("(tc p) f -> p tc f", p=P)
                for part in range(4):
                    c_kv_load[hs] = do(
                        "sync", lambda x, o=bf(O_VH + hs * 16384 + part * 4096, 2048
                                               ).rearrange("p (tc f) -> p tc f", tc=16),
                        i=vsrc[:, part * 16:(part + 1) * 16, h * P:(h + 1) * P]:
                        x.dma_start(out=o, in_=i), "kv0" if hs == 0 else "kv1")
                for u in range(16):
                    us = u % 2
                    up = u % 2
                    pvp = pvp2[up]
                    pzp = pzp2[up]
                    qt = bf(O_QT + us * 1024, 512)
                    wait("sync", "pe", c_qt_used[us])
                    c_qt_load[us] = do(
                        "sync", lambda x, o=qt,
                        i=q_slab[h * P:(h + 1) * P, u * 512:(u + 1) * 512]:
                        x.dma_start(out=o, in_=i), "q0" if us == 0 else "q1")

                    c_l1 = None
                    c_pv = None
                    c_st_last = None
                    c_lacc = 0
                    c_expg = [0] * 32
                    lacc = f32(O_L + 8192 + up * 2048, 512)
                    for g in range(34):
                        if g < 32:
                            ps = psx[g % 2]
                            pt = bf(178176 + (g % 8) * 2048, 1024)
                            wait("tensor", "act", c_exp2[g % 2])
                            if first_unit and g < 2:
                                wait("tensor", "dve", c_p1_clear)
                            if g == 0:
                                wait("tensor", "kv0" if hs == 0 else "kv1",
                                     c_kv_load[hs])
                                wait("tensor", "q0" if us == 0 else "q1",
                                     c_qt_load[us])
                                if first_unit:
                                    wait("tensor", "dve", c_consts)
                                    wait("scalar", "dve", c_consts)
                            for jj in range(2):
                                kc = 2 * g + jj
                                c_st_last = do(
                                    "tensor",
                                    lambda x, o=ps[:, jj * 512:(jj + 1) * 512],
                                    l=ksb[:, kc * P:(kc + 1) * P], r=qt:
                                    x.matmul(o, l, r, start=True, stop=True),
                                    "pe")
                            wait("scalar", "pe", c_st_last)
                            ce = do("scalar", lambda x, o=pt, i=ps[:, :]:
                                    x.activation(o, i, AF.Exp, bias=negC,
                                                 scale=INV_SQRT_HD), "act")
                            c_exp2[g % 2] = ce
                            c_expg[g] = ce
                        if g >= 2:
                            gp = g - 2
                            ptp = bf(178176 + (gp % 8) * 2048, 1024)
                            if g >= 32:
                                wait("tensor", "act", c_expg[gp])
                            if gp == 0:
                                wait("tensor", "dve", c_att[up])
                                wait("tensor", "dve", c_rbe[up])
                            for jj in range(2):
                                kc = 2 * gp + jj
                                c_pv = do("tensor", lambda x, o=pvp,
                                          l=vsb[:, kc * P:(kc + 1) * P],
                                          r=ptp[:, jj * 512:(jj + 1) * 512],
                                          s=(gp == 0 and jj == 0),
                                          t=(gp == 31 and jj == 1):
                                          x.matmul(o, l, r, start=s, stop=t),
                                          "pe" if (gp == 31 and jj == 1) else None)
                            c_l1 = do("tensor", lambda x, o=pzp[0:1, :],
                                      l=ones_bf, r=ptp[:, 0:512],
                                      s=(gp == 0), t=False:
                                      x.matmul(o, l, r, start=s, stop=t),
                                      "pe" if gp == 31 else None)
                            wait("vector", "act", c_expg[gp])
                            if gp == 0:
                                wait("vector", "pe", c_lacc_freed[up])
                                cla = do("vector", lambda x, o=lacc,
                                         i=ptp[:, 512:1024]:
                                         x.tensor_copy(o, i), "dve")
                            else:
                                wait("vector", "dve", c_lacc)
                                cla = do("vector", lambda x, o=lacc, a=lacc,
                                         i=ptp[:, 512:1024]:
                                         x.tensor_add(o, a, i), "dve")
                            c_lacc = cla
                    first_unit = False
                    c_qt_used[us] = c_st_last
                    if u == 15:
                        c_kv_used[hs] = c_pv

                    l_sb = f32(O_L + up * 2048, 512, parts=1)
                    r_sb = f32(O_L + 4096 + up * 2048, 512, parts=1)
                    rb_sb = f32(O_L + 12288 + up * 2048, 512)
                    ath = bf(O_ATH, 8192)
                    wait("tensor", "dve", c_lacc)
                    c_lmm2 = do("tensor", lambda x, o=pzp[0:1, :], l=onescol,
                                r=lacc: x.matmul(o, l, r, start=False,
                                                 stop=True), "pe")
                    c_lacc_freed[up] = c_lmm2
                    wait("vector", "pe", c_lmm2)
                    cle = do("vector", lambda x, o=l_sb, i=pzp[0:1, :]:
                             x.tensor_copy(o, i), "dve")
                    wait("vector", "dve", cle)
                    crr = do("vector", lambda x, o=r_sb, i=l_sb:
                             x.reciprocal(o, i), "dve")
                    wait("tensor", "dve", crr)
                    crb = do("tensor", lambda x, o=pzp, l=ones1f, r=r_sb:
                             x.matmul(o, l, r, start=True, stop=True), "pe")
                    wait("vector", "pe", crb)
                    c_rbe[up] = do("vector", lambda x, o=rb_sb, i=pzp:
                                   x.tensor_copy(o, i), "dve")
                    wait("vector", "dve", c_rbe[up])
                    wait("vector", "pe", c_pv)
                    if u == 0:
                        wait("vector", "st_a", c_attdma)
                    c_att[up] = do(
                        "vector", lambda x, o=ath[:, u * 512:(u + 1) * 512],
                        i=pvp, b=rb_sb:
                        x.scalar_tensor_tensor(o, i, 1.0, b, ALU.mult,
                                               ALU.mult), "dve")
                wait("gpsimd", "dve", max(c_att))
                c_attdma = do("gpsimd", lambda x,
                              o=attn_l[h * P:(h + 1) * P, :],
                              i=bf(O_ATH, 8192): x.dma_start(out=o, in_=i),
                              "st_a")

            if nphases >= 3 and "nocoll" not in dbg:
                wait("gpsimd", "st_a", ctr["st_a"].n)
                c_gather = do("gpsimd", lambda x: x.collective_compute(
                    "AllGather", ALU.bypass,
                    replica_groups=[[0, 1, 2, 3], [4, 5, 6, 7]],
                    ins=[attn_l.opt()], outs=[attn_g.opt()]), "cc")
            elif nphases >= 3:
                wait("gpsimd", "st_a", ctr["st_a"].n)
                c_fakeg = do("gpsimd", lambda x: x.dma_start(
                    out=attn_g[0:FSL, :], in_=attn_l[:, :]), "st_a")

        if nphases >= 3:
            B3 = B1
            O_WUC = B3
            O_WUS = B3 + 16384
            O_WUD = B3 + 32768
            O_X2S = B3 + 40960
            O_TUP = B3 + 49152
            O_AUPB = B3 + 53248
            O_BUPL = B3 + 53760
            O_BUPT = B3 + 70144
            O_LST = B3 + 70400
            assert O_LST + 16384 <= 2 * NA

            wait("sync", "dve", ctr["dve"].n)
            wait("sync", "act", ctr["act"].n)
            wait("sync", "pe", ctr["pe"].n)
            wait("sync", "st_a", ctr["st_a"].n)

            xqr = bf(O_XQ, 16 * 2048).rearrange("p (dc t) -> p dc t", dc=16)
            aupb = bf(O_AUPB, 256)
            buplb = bf(O_BUPL, 8192, parts=16)
            bup_sb = f32(O_BUPT, 64)
            tup = bf(O_TUP, 2048, parts=16)

            stg1 = f32(O_LST, 256)
            ca = do("sync", lambda x,
                    o=f32(O_LST, 256).rearrange("p (dc r) -> p dc r", dc=16),
                    i=aup.rearrange("(dc p) r -> p dc r", p=P):
                    x.dma_start(out=o, in_=i), "wu0")
            wait("vector", "wu0", ca)
            c_aupb = do("vector", lambda x, o=aupb, i=stg1:
                        x.tensor_copy(o, i), "dve")
            stg2 = f32(O_LST, 4096, parts=16)
            c_bup = 0
            for chh in range(2):
                wait("sync", "dve", c_bup if chh else c_aupb)
                cb = do("sync", lambda x, o=stg2,
                        i=bupl[:, chh * 4096:(chh + 1) * 4096]:
                        x.dma_start(out=o, in_=i), "wu0")
                wait("vector", "wu0", cb)
                c_bup = do("vector", lambda x,
                           o=buplb[:, chh * 4096:(chh + 1) * 4096], i=stg2:
                           x.tensor_copy(o, i), "dve")
            wait("sync", "dve", c_bup)
            cbt = do("sync", lambda x, o=bup_sb, i=bup_t:
                     x.dma_start(out=o, in_=i), "wu0")

            c_tup = 0
            for qq in range(4):
                wait("tensor", "dve", c_aupb)
                bank_wait("tensor", ("pz",))
                cm = None
                for dc in range(16):
                    cm = do("tensor", lambda x, o=pz[0:16, :],
                            l=aupb[:, dc * 16:(dc + 1) * 16],
                            r=xqr[:, dc, qq * 512:(qq + 1) * 512],
                            s=(dc == 0), t=(dc == 15):
                            x.matmul(o, l, r, start=s, stop=t),
                            "pe" if dc == 15 else None)
                wait("vector", "pe", cm)
                c_tup = do("vector", lambda x,
                           o=tup[:, qq * 512:(qq + 1) * 512], i=pz[0:16, :]:
                           x.tensor_copy(o, i), "dve")
                bank_set(("pz",), "dve", c_tup)

            wait("sync", "dve", c_tup)
            wait("sync", "wu0", cbt)
            c_wu_load = [0, 0]
            c_wu_used = [0, 0]
            c_wud_used = [0, 0]
            c_x2dma = 0
            c_x2evac = [0, 0]
            c_up_mm = 0
            wu_sem = ("wu0", "wu1")
            for hc in range(64):
                slh = hc % 2
                codes_t = i32(O_WUC + slh * 8192, 2048)
                sce_t = f32(O_WUS + slh * 8192, 2048)
                wud = bf(O_WUD + slh * 4096, 2048)
                wait("sync", "dve", c_wu_used[slh])
                do("sync", lambda x, o=codes_t,
                   i=wupT_q.rearrange("(dc p) h -> p dc h", p=P)[
                       :, :, hc * P:(hc + 1) * P]:
                   x.dma_start(out=o, in_=i), wu_sem[slh])
                c_wu_load[slh] = do(
                    "sync", lambda x, o=sce_t,
                    i=wup_se.rearrange("(dc p) h -> p dc h", p=P)[
                        :, :, hc * P:(hc + 1) * P]:
                    x.dma_start(out=o, in_=i), wu_sem[slh])
                wait("vector", wu_sem[slh], c_wu_load[slh])
                wait("vector", "pe", c_wud_used[slh])
                c_wu_used[slh] = do(
                    "vector", lambda x, o=wud, a=codes_t, b=sce_t:
                    x.scalar_tensor_tensor(o, a, 7.5, b, ALU.subtract,
                                           ALU.mult), "dve")
                x2st = bf(O_X2S + slh * 4096, 2048)
                wudr = wud.rearrange("p (dc h) -> p dc h", dc=16)
                for qq in range(4):
                    ps = (px0, px1)[qq % 2]
                    hfo = (qq // 2) * 512
                    pv = ps[:, hfo:hfo + 512]
                    key = (id(ps), qq // 2)
                    if qq == 0:
                        wait("tensor", "dve", max(c_wu_used[slh], c_bup, c_tup))
                    bank_wait("tensor", key)
                    cm = None
                    for dc in range(16):
                        cm = do("tensor", lambda x, o=pv, l=wudr[:, dc, :],
                                r=xqr[:, dc, qq * 512:(qq + 1) * 512],
                                s=(dc == 0), t=False:
                                x.matmul(o, l, r, start=s, stop=t), None)
                    cm = do("tensor", lambda x, o=pv,
                            l=buplb[:, hc * P:(hc + 1) * P],
                            r=tup[:, qq * 512:(qq + 1) * 512]:
                            x.matmul(o, l, r, start=False, stop=True), "pe")
                    c_up_mm = cm
                    c_wud_used[slh] = cm
                    wait("scalar", "pe", cm)
                    if hc == 0 and qq == 0:
                        wait("scalar", "wu0", cbt)
                    if qq == 0:
                        wait("scalar", "st_x2", c_x2dma)
                    ce = do("scalar", lambda x,
                            o=x2st[:, qq * 512:(qq + 1) * 512], i=pv,
                            b=bup_sb[:, hc:hc + 1]:
                            x.activation(o, i, AF.Relu, bias=b, scale=1.0),
                            "act")
                    c_x2evac[slh] = ce
                    bank_set(key, "act", ce)
                wait("gpsimd", "act", c_x2evac[slh])
                c_x2dma = do("gpsimd", lambda x,
                             o=x2_slab[hc * P:(hc + 1) * P, :], i=x2st:
                             x.dma_start(out=o, in_=i), "st_x2")

            O_Y2A = 4096
            O_X2B = 4096 + 65536
            B3d = O_X2B + 65536
            O_WDC = B3d
            O_WDS = B3d + 16384
            O_WDD = B3d + 32768
            O_YST = B3d + 49152
            O_T2 = B3d + 53248
            O_ADNB = B3d + 57344
            O_BDNL = B3d + 59392
            assert O_BDNL + 4096 <= 2 * NA

            wait("sync", "dve", ctr["dve"].n)
            wait("sync", "act", ctr["act"].n)
            wait("sync", "pe", ctr["pe"].n)
            wait("sync", "st_x2", ctr["st_x2"].n)

            t2 = bf(O_T2, 2048, parts=16)
            adnb = bf(O_ADNB, 1024)
            bdnlb = bf(O_BDNL, 2048, parts=16)

            stga = f32(O_WDC, 1024)
            cda = do("sync", lambda x,
                     o=f32(O_WDC, 1024).rearrange("p (hc r) -> p hc r", hc=64),
                     i=adn.rearrange("(hc p) r -> p hc r", p=P):
                     x.dma_start(out=o, in_=i), "wd0")
            wait("vector", "wd0", cda)
            c_adnb = do("vector", lambda x, o=adnb, i=stga:
                        x.tensor_copy(o, i), "dve")
            stgb = f32(O_WDC, 2048, parts=16)
            wait("sync", "dve", c_adnb)
            cdb = do("sync", lambda x, o=stgb, i=bdnl[:, :]:
                     x.dma_start(out=o, in_=i), "wd0")
            wait("vector", "wd0", cdb)
            c_bdnlb = do("vector", lambda x, o=bdnlb, i=stgb:
                         x.tensor_copy(o, i), "dve")

            t2ps = (pz[0:16, :], pv_t[0:16, :],
                    px2[0:16, 0:512], px2[0:16, 512:1024])
            t2keys = (("pz",), ("pv_t",), (id(px2), 0), (id(px2), 1))
            c_x2b_used = [0, 0]
            c_x2b_load = [0, 0]
            c_wd_load = [0, 0]
            c_wd_used = [0, 0]
            c_wdd_used = [0, 0]
            c_y2add = 0
            c_ydma = 0
            wd_sem = ("wd0", "wd1")
            ch_ct = 0
            c_t2e = 0
            blk = 0
            for dh in range(2):
                for hb in range(8):
                    xsl = blk % 2
                    blk += 1
                    x2b = bf(O_X2B + xsl * 32768, 8 * 2048)
                    wait("sync", "pe", c_x2b_used[xsl])
                    if dh == 0 and hb == 0:
                        wait("sync", "dve", c_bdnlb)
                    c_x2b_load[xsl] = do(
                        "sync", lambda x,
                        o=x2b.rearrange("p (hc q) -> p hc q", hc=8),
                        i=x2_slab.rearrange("(hb hc p) q -> p hb hc q", p=P,
                                            hc=8)[:, hb, :, :]:
                        x.dma_start(out=o, in_=i), "x2b" if xsl == 0 else "x2c")
                    for chk in range(4):
                        slw = ch_ct % 2
                        ch_ct += 1
                        cod = i32(O_WDC + slw * 8192, 2048)
                        sce = f32(O_WDS + slw * 8192, 2048)
                        wait("sync", "dve", c_wd_used[slw])
                        do("sync", lambda x, o=cod,
                           i=wdnT_q.rearrange("(hb hc p) d -> p hb hc d",
                                              p=P, hc=8)[
                               :, hb, 2 * chk:2 * (chk + 1),
                               dh * 1024:(dh + 1) * 1024]:
                           x.dma_start(out=o, in_=i), wd_sem[slw])
                        c_wd_load[slw] = do(
                            "sync", lambda x, o=sce,
                            i=wdn_se.rearrange("(hb hc p) d -> p hb hc d",
                                               p=P, hc=8)[
                                :, hb, 2 * chk:2 * (chk + 1),
                                dh * 1024:(dh + 1) * 1024]:
                            x.dma_start(out=o, in_=i), wd_sem[slw])
                        wait("vector", wd_sem[slw], c_wd_load[slw])
                        wait("vector", "pe", c_wdd_used[1] if ch_ct > 4 else 0)
                        c_wd_used[slw] = do(
                            "vector", lambda x,
                            o=bf(O_WDD + chk * 4096, 2048),
                            a=cod, b=sce:
                            x.scalar_tensor_tensor(o, a, 7.5, b, ALU.subtract,
                                                   ALU.mult), "dve")
                    c_wdeq = c_wd_used[(ch_ct - 1) % 2]

                    x2br = x2b.rearrange("p (hc q) -> p hc q", hc=8)
                    wddr = bf(O_WDD, 8 * 1024).rearrange("p (hc d) -> p hc d",
                                                         hc=8)
                    if dh == 0:
                        wait("tensor", "x2b" if xsl == 0 else "x2c",
                             c_x2b_load[xsl])
                        wait("tensor", "dve", c_adnb)
                        if hb == 0:
                            for kk in t2keys:
                                bank_wait("tensor", kk)
                        for hc2 in range(8):
                            hcg = hb * 8 + hc2
                            for qq in range(4):
                                cm = do("tensor", lambda x, o=t2ps[qq],
                                        l=adnb[:, hcg * 16:(hcg + 1) * 16],
                                        r=x2br[:, hc2, qq * 512:(qq + 1) * 512],
                                        s=(hb == 0 and hc2 == 0),
                                        t=(hb == 7 and hc2 == 7):
                                        x.matmul(o, l, r, start=s, stop=t),
                                        "pe" if (hb == 7 and hc2 == 7 and qq == 3)
                                        else None)
                        if hb == 7:
                            c_t2mm = cm
                    wait("tensor", "x2b" if xsl == 0 else "x2c",
                         c_x2b_load[xsl])
                    wait("tensor", "dve", c_wdeq)
                    cm = None
                    for tc in range(16):
                        for d5 in range(2):
                            ps = (px0, px1)[(tc * 2 + d5) % 2]
                            hfo = (((tc * 2 + d5) // 2) % 2) * 512
                            pv = ps[:, hfo:hfo + 512]
                            key = (id(ps), hfo // 512)
                            bank_wait("tensor", key)
                            for hc2 in range(8):
                                cm = do("tensor", lambda x, o=pv,
                                        l=x2br[:, hc2, tc * P:(tc + 1) * P],
                                        r=wddr[:, hc2, d5 * 512:(d5 + 1) * 512],
                                        s=(hc2 == 0), t=(hc2 == 7):
                                        x.matmul(o, l, r, start=s, stop=t),
                                        "pe" if hc2 == 7 else None)
                            wait("vector", "pe", cm)
                            dsl = f32(O_Y2A + tc * 4096 + d5 * 2048, 512)
                            if hb == 0:
                                cy = do("vector", lambda x, o=dsl, i=pv:
                                        x.tensor_copy(o, i), "dve")
                            else:
                                cy = do("vector", lambda x, o=dsl, a=dsl, i=pv:
                                        x.tensor_add(o, a, i), "dve")
                            c_y2add = cy
                            bank_set(key, "dve", cy)
                    c_x2b_used[xsl] = cm
                    c_wdd_used[1] = cm

                if dh == 0:
                    wait("vector", "pe", c_t2mm)
                    for qq in range(4):
                        c_t2e = do("vector", lambda x,
                                   o=t2[:, qq * 512:(qq + 1) * 512],
                                   i=t2ps[qq]: x.tensor_copy(o, i), "dve")
                        bank_set(t2keys[qq], "dve", c_t2e)

                for tc in range(16):
                    for d5 in range(2):
                        ps = (px0, px1)[(tc * 2 + d5) % 2]
                        hfo = (((tc * 2 + d5) // 2) % 2) * 512
                        pv = ps[:, hfo:hfo + 512]
                        key = (id(ps), hfo // 512)
                        bank_wait("tensor", key)
                        if tc == 0 and d5 == 0:
                            wait("tensor", "dve", max(c_t2e, c_bdnlb))
                        cm = do("tensor", lambda x, o=pv,
                                l=t2[:, tc * P:(tc + 1) * P],
                                r=bdnlb[:, dh * 1024 + d5 * 512:
                                        dh * 1024 + (d5 + 1) * 512]:
                                x.matmul(o, l, r, start=True, stop=True), "pe")
                        wait("vector", "pe", cm)
                        dsl = f32(O_Y2A + tc * 4096 + d5 * 2048, 512)
                        cy = do("vector", lambda x, o=dsl, a=dsl, i=pv:
                                x.tensor_add(o, a, i), "dve")
                        bank_set(key, "dve", cy)
                    ys = bf(O_YST + (tc % 2) * 2048, 1024)
                    wait("vector", "dve", cy)
                    wait("vector", "st_y2", c_ydma)
                    cyv = do("vector", lambda x, o=ys,
                             i=f32(O_Y2A + tc * 4096, 1024):
                             x.tensor_copy(o, i), "dve")
                    wait("gpsimd", "dve", cyv)
                    c_ydma = do("gpsimd", lambda x,
                                o=y2_slab[tc * P:(tc + 1) * P,
                                          dh * 1024:(dh + 1) * 1024], i=ys:
                                x.dma_start(out=o, in_=i), "st_y2")

        if nphases >= 4:
            B4 = 2048
            O_WOD = B4
            O_WOC = B4 + 65536
            O_WOS = B4 + 81920
            O_AG = B4 + 98304
            O_XR = B4 + 106496
            O_YR = B4 + 122880
            O_OST = B4 + 131072
            O_XY = B4 + 147456
            O_BO = B4 + 163840
            assert O_BO + 24576 <= 2 * NA

            wait("sync", "dve", ctr["dve"].n)
            wait("sync", "act", ctr["act"].n)
            wait("sync", "pe", ctr["pe"].n)
            wait("sync", "st_y2", ctr["st_y2"].n)
            if "nocoll" not in dbg:
                wait("sync", "cc", 1)
            else:
                wait("sync", "st_a", ctr["st_a"].n)

            wod = bf(O_WOD, 16 * 2048)
            bo_sb = f32(O_BO, 2048)
            bdn_sb = f32(O_BO + 8192, 2048)
            bb = f32(O_BO + 16384, 2048)
            do("sync", lambda x: x.dma_start(out=bo_sb, in_=bo_b), "wo_s")
            cbb = do("sync", lambda x: x.dma_start(out=bdn_sb, in_=bdn_b), "wo_s")
            wait("vector", "wo_s", cbb)
            c_bb = do("vector", lambda x: x.tensor_add(bb, bo_sb, bdn_sb), "dve")
            wait("sync", "dve", c_bb)

            c_wo_load = [0, 0]
            c_wo_used = [0, 0]
            c_wo_mult = 0
            wo_sem = ("wu0", "wu1")
            for dc in range(16):
                slw = dc % 2
                codt = f32(O_WOC + slw * 8192, 2048)
                scet = f32(O_WOS + slw * 8192, 2048)
                wait("sync", "dve", c_wo_used[slw])
                do("sync", lambda x, o=codt,
                   i=wo_c.rearrange("(dc p) d -> p dc d", p=P)[:, dc, :]:
                   x.dma_start(out=o, in_=i), wo_sem[slw])
                c_wo_load[slw] = do(
                    "sync", lambda x, o=scet,
                    i=wo_se.rearrange("(dc p) d -> p dc d", p=P)[:, dc, :]:
                    x.dma_start(out=o, in_=i), wo_sem[slw])
                wait("vector", wo_sem[slw], c_wo_load[slw])
                c_wo_mult = do("vector", lambda x,
                               o=wod[:, dc * 2048:(dc + 1) * 2048],
                               a=codt, b=scet: x.tensor_mul(o, a, b), "dve")
                c_wo_used[slw] = c_wo_mult

            wodr = wod.rearrange("p (dc d) -> p dc d", dc=16)
            qoff_sb = i32(O_BO + 24576, 1, parts=1)
            cqo = do("sync", lambda x: x.dma_start(out=qoff_sb, in_=qoff), "wo_s")
            wait("sync", "wo_s", cqo)
            do("sync", lambda x: x.reg_load(REGS["rb"], qoff_sb))
            agt = attn_g.tensor
            c_pa_load = [0, 0]
            c_pa_used = [0, 0]
            c_xy = [0, 0]
            c_ost = [0, 0]
            c_odma = 0
            for tk in range(16):
                slp = tk % 2
                ag = bf(O_AG + slp * 4096, 2048)
                xr = f32(O_XR + slp * 8192, 2048)
                yr = bf(O_YR + slp * 4096, 2048)
                xy = f32(O_XY + slp * 8192, 2048)
                ost = f32(O_OST + slp * 8192, 2048)
                pa = "pa0" if slp == 0 else "pa1"
                wait("sync", "pe", c_pa_used[slp])
                wait("sync", "dve", c_xy[slp])
                do("sync", lambda x, t=tk:
                   x.reg_alu(REGS["ra"], REGS["rb"], t * P, ALU.add))
                do("sync", lambda x, o=ag.rearrange("p (fc t) -> p fc t", fc=16):
                   x.dma_start(out=o, in_=bass.AP(
                       agt, REGS["ra"], [[S, P], [P * S, 16], [1, P]])), pa)
                do("sync", lambda x, o=xr, i=x_q[tk * P:(tk + 1) * P, :]:
                   x.dma_start(out=o, in_=i), pa)
                c_pa_load[slp] = do(
                    "sync", lambda x, o=yr, i=y2_slab[tk * P:(tk + 1) * P, :]:
                    x.dma_start(out=o, in_=i), pa)

                wait("vector", pa, c_pa_load[slp])
                wait("vector", "dve", c_bb)
                cx1 = do("vector", lambda x, o=xy, a=xr, b=yr:
                         x.tensor_add(o, a, b), "dve")
                wait("vector", "dve", cx1)
                cx2 = do("vector", lambda x, o=xy, a=xy, b=bb:
                         x.tensor_add(o, a, b), "dve")
                c_xy[slp] = cx2

                cm_last = 0
                for d5 in range(4):
                    ps = (px0, px1)[d5 % 2]
                    hfo = (d5 // 2) * 512
                    pv = ps[:, hfo:hfo + 512]
                    key = (id(ps), hfo // 512)
                    if tk == 0 and d5 == 0:
                        wait("tensor", "dve", c_wo_mult)
                    if d5 == 0:
                        wait("tensor", pa, c_pa_load[slp])
                    bank_wait("tensor", key)
                    cm = None
                    for fc in range(16):
                        cm = do("tensor", lambda x, o=pv,
                                l=ag[:, fc * P:(fc + 1) * P],
                                r=wodr[:, fc, d5 * 512:(d5 + 1) * 512],
                                s=(fc == 0), t=(fc == 15):
                                x.matmul(o, l, r, start=s, stop=t),
                                "pe" if fc == 15 else None)
                    cm_last = cm
                    wait("vector", "pe", cm)
                    wait("vector", "dve", c_xy[slp])
                    if tk >= 2 and d5 == 0:
                        wait("vector", "st_o", c_odma)
                    co = do("vector", lambda x,
                            o=ost[:, d5 * 512:(d5 + 1) * 512], i=pv,
                            b=xy[:, d5 * 512:(d5 + 1) * 512]:
                            x.scalar_tensor_tensor(o, i, 1.0, b, ALU.mult,
                                                   ALU.add), "dve")
                    c_ost[slp] = co
                    bank_set(key, "dve", co)
                c_pa_used[slp] = cm_last
                wait("gpsimd", "dve", c_ost[slp])
                c_odma = do("gpsimd", lambda x,
                            o=out[tk * P:(tk + 1) * P, :], i=ost:
                            x.dma_start(out=o, in_=i), "st_o")

        for snm in DMA_SEMS:
            if snm.startswith("st_") and ctr[snm].n:
                wait("gpsimd", snm, ctr[snm].n)

        block = ctx.enter_context(nc.Block())
        REGS = {}

        @block.sync
        def _(sync):
            with sync.register("rb") as rb, sync.register("ra") as ra:
                REGS["rb"] = rb
                REGS["ra"] = ra
                for f in ops["sync"]:
                    f(sync)

        @block.tensor
        def _(tensor):
            for f in ops["tensor"]:
                f(tensor)

        @block.vector
        def _(vector):
            for f in ops["vector"]:
                f(vector)

        @block.scalar
        def _(scalar):
            for f in ops["scalar"]:
                f(scalar)

        @block.gpsimd
        def _(gpsimd):
            for f in ops["gpsimd"]:
                f(gpsimd)

    return nc



def _prep_inputs(inputs):
    x = inputs["L0_attn_x"]
    g = {}
    g["wo_c"] = np.ascontiguousarray(inputs["L0_attn_wo_codes"])
    g["wo_se"] = np.repeat(inputs["L0_attn_wo_scales"], 32, axis=0)
    g["bo_b"] = np.broadcast_to(inputs["L0_attn_bo"], (P, D)).copy()
    g["bdn_b"] = np.broadcast_to(inputs["L0_ffn_b_down"], (P, D)).copy()
    g["bup_t"] = np.ascontiguousarray(inputs["L0_ffn_b_up"].reshape(64, P).T)
    g["wupT_q"] = np.ascontiguousarray(inputs["L0_ffn_w_up_q"].T)
    g["wup_se"] = np.ascontiguousarray(
        np.repeat(inputs["L0_ffn_w_up_scale"], 64, axis=1).T)
    g["wdnT_q"] = np.ascontiguousarray(inputs["L0_ffn_w_down_q"].T)
    g["wdn_se"] = np.ascontiguousarray(
        np.repeat(inputs["L0_ffn_w_down_scale"], 64, axis=1).T)
    g["aup"] = inputs["L0_ffn_w_up_lora_a"]
    g["bupl"] = inputs["L0_ffn_w_up_lora_b"]
    g["adn"] = inputs["L0_ffn_w_down_lora_a"]
    g["bdnl"] = inputs["L0_ffn_w_down_lora_b"]
    g["eye_bf"] = np.eye(P, dtype=ml_dtypes.bfloat16)

    in_maps = []
    for c in range(NCORE):
        b, r = c // 4, c % 4
        sl = slice(r * FSL, (r + 1) * FSL)
        m = dict(g)
        m["x_b"] = x[b]
        m["x_q"] = np.ascontiguousarray(x[b, r * QTOK:(r + 1) * QTOK, :])
        for nm in ("q", "k", "v"):
            m[f"w{nm}_c"] = np.ascontiguousarray(
                inputs[f"L0_attn_w{nm}_codes"][:, sl])
            m[f"w{nm}_se"] = np.repeat(
                inputs[f"L0_attn_w{nm}_scales"][:, sl], 32, axis=0)
        m["bq_t"] = np.ascontiguousarray(inputs["L0_attn_bq"][sl].reshape(4, P).T)
        m["bk_t"] = np.ascontiguousarray(inputs["L0_attn_bk"][sl].reshape(4, P).T)
        m["bv_b"] = np.broadcast_to(inputs["L0_attn_bv"][sl], (P, FSL)).copy()
        m["qoff"] = np.array([[r * QTOK]], dtype=np.int32)
        in_maps.append(m)
    return in_maps


def kernel(**inputs):
    in_maps = _prep_inputs(inputs)
    nc = build_program()
    res = run_bass_kernel_spmd(nc, in_maps, core_ids=list(range(NCORE)))
    out = np.empty((2, S, D), dtype=np.float32)
    for c in range(NCORE):
        b, r = c // 4, c % 4
        out[b, r * QTOK:(r + 1) * QTOK, :] = res.results[c]["out"]
    return out
